# version 20
# baseline (speedup 1.0000x reference)
import numpy as np
import ml_dtypes
import concourse.bass as bass
import concourse.mybir as mybir
from concourse.bass_utils import run_bass_kernel_spmd
from contextlib import ExitStack

F32 = mybir.dt.float32
BF16 = mybir.dt.bfloat16
AF = mybir.ActivationFunctionType
ALU = mybir.AluOpType
AX = mybir.AxisListType

DEBUG_DUMP = False
USE_POOL_TREE = True
NCORES = 8
BPC = 2
L = 2048
DIM = 1024
INNER = 2048
NST = 16
NEXP = 8
TC = 256
NSTAGE = BPC * (L // TC)
KD = DIM // 128
KI = INNER // 128
EPS = 1e-5


def build_nc():
    nc = bass.Bass("TRN2", target_bir_lowering=False)

    d_x = nc.dram_tensor("x", [BPC, DIM, L], F32, kind="ExternalInput").ap()
    d_ipjT = nc.dram_tensor("ipjT", [DIM, 2 * INNER], F32, kind="ExternalInput").ap()
    d_convw = nc.dram_tensor("convw", [INNER, 4], F32, kind="ExternalInput").ap()
    d_convb = nc.dram_tensor("convb", [INNER], F32, kind="ExternalInput").ap()
    d_dtwT = nc.dram_tensor("dtwT", [INNER, INNER], F32, kind="ExternalInput").ap()
    d_dtb = nc.dram_tensor("dtb", [INNER], F32, kind="ExternalInput").ap()
    d_alog = nc.dram_tensor("alog", [INNER, NST], F32, kind="ExternalInput").ap()
    d_BwT = nc.dram_tensor("BwT", [INNER, NST], F32, kind="ExternalInput").ap()
    d_CwT = nc.dram_tensor("CwT", [INNER, NST], F32, kind="ExternalInput").ap()
    d_outwT = nc.dram_tensor("outwT", [INNER, DIM], F32, kind="ExternalInput").ap()
    d_rw = nc.dram_tensor("rw", [DIM, NEXP], F32, kind="ExternalInput").ap()
    d_rb = nc.dram_tensor("rb", [1, NEXP], F32, kind="ExternalInput").ap()
    d_ew = nc.dram_tensor("ew", [NEXP, DIM, DIM], F32, kind="ExternalInput").ap()
    d_eb = nc.dram_tensor("eb", [NEXP, DIM], F32, kind="ExternalInput").ap()
    d_eye = nc.dram_tensor("eye", [128, 128], F32, kind="ExternalInput").ap()
    d_ones1f = nc.dram_tensor("ones1f", [1, 128], F32, kind="ExternalInput").ap()
    d_onescol = nc.dram_tensor("onescol", [128, 1], BF16, kind="ExternalInput").ap()
    d_onesrow = nc.dram_tensor("onesrow", [1, TC], BF16, kind="ExternalInput").ap()
    d_cvals = nc.dram_tensor("cvals", [128, 4], F32, kind="ExternalInput").ap()

    d_out = nc.dram_tensor("out", [BPC, DIM, L], F32, kind="ExternalOutput").ap()
    if DEBUG_DUMP:
        d_dbg1 = nc.dram_tensor("dbg1", [BPC, DIM, L], F32, kind="ExternalOutput").ap()
        d_dbg2 = nc.dram_tensor("dbg2", [BPC, DIM, L], F32, kind="ExternalOutput").ap()
        d_dbg3 = nc.dram_tensor("dbg3", [NSTAGE, NEXP, TC], BF16, kind="ExternalOutput").ap()
        d_dbg4 = nc.dram_tensor("dbg4", [NSTAGE, 1, NEXP * TC], BF16, kind="ExternalOutput").ap()
        d_dbg5 = nc.dram_tensor("dbg5", [4, 128, KI * TC], BF16, kind="ExternalOutput").ap()

    s_ipj = nc.dram_tensor("s_ipj", [32, 128, KD * 128], BF16, kind="Internal").ap()
    s_dtw = nc.dram_tensor("s_dtw", [16, 128, KI * 128], BF16, kind="Internal").ap()
    s_outw = nc.dram_tensor("s_outw", [16, 128, KD * 128], BF16, kind="Internal").ap()
    s_exp = nc.dram_tensor("s_exp", [NEXP, KD, 128, DIM], BF16, kind="Internal").ap()
    s_mg0 = nc.dram_tensor("s_mg0", [NEXP, TC], BF16, kind="Internal").ap()
    s_mg1 = nc.dram_tensor("s_mg1", [NEXP, TC], BF16, kind="Internal").ap()
    s_bmc = nc.dram_tensor("s_bmc", [NST, 2 * TC], BF16, kind="Internal").ap()

    stack = ExitStack()
    SB = lambda name, shape, dt: stack.enter_context(nc.sbuf_tensor(name, shape, dt))
    PS = lambda name: stack.enter_context(nc.psum_tensor(name, [128, 512], F32))

    residR = [SB(f"residR{i}", [128, KD * TC], F32) for i in range(4)]
    scr2 = SB("scr2", [128, 2 * KD * TC], BF16)
    scr2b = SB("scr2b", [128, KD * TC], BF16)
    xln0 = scr2[:, 0:KD * TC]
    xln1 = scr2b[:, :]
    carry = SB("carry", [128, 16 * 4], BF16)
    xs = SB("xs", [128, KI * TC], F32)
    xsbf = SB("xsbf", [128, KI * TC], BF16)
    dtb_ = SB("dt", [128, KI * TC], BF16)
    ubf = SB("ubf", [128, KI * TC], BF16)
    sres = SB("sres", [128, KI * TC], BF16)
    oop = SB("oop", [128, KI * TC], BF16)
    egb = SB("egb", [128, 3 * TC], F32)
    wbuf = SB("wbuf", [128, NST * TC], BF16)
    hbuf = SB("hbuf", [128, NST * TC], BF16)
    ypr = SB("ypr", [128, NST * TC], BF16)
    yrow = SB("yrow", [128, TC], F32)
    state = SB("state", [128, BPC * 16 * NST], F32)
    brep = SB("brep", [128, NST * TC], BF16)
    crep = SB("crep", [128, NST * TC], BF16)
    spscr = SB("spscr", [128, 6 * TC], F32)
    yrow2 = SB("yrow2", [128, TC], F32)
    xe_sb = SB("xe_sb", [128, 2 * NEXP * TC], BF16)
    lncast = SB("lncast", [128, 2 * KD * TC], BF16)
    stats_t = SB("stats_t", [1, 1024], F32)
    xtmp = xs[:, 0:KD * TC]
    ipj_w = [SB(f"ipj_w{i}", [128, KD * 128], BF16) for i in range(2)]
    dtw_w = [SB(f"dtw_w{i}", [128, KD * 128], BF16) for i in range(3)]
    outw_w = [SB(f"outw_w{i}", [128, KD * 128], BF16) for i in range(3)]
    exp_w = [SB(f"exp_w{i}", [128, DIM], BF16) for i in range(3)]
    rw_bf = SB("rw_bf", [128, KD * NEXP], BF16)
    rb_bf = SB("rb_bf", [1, NEXP], BF16)
    eb_bf = SB("eb_bf", [NEXP, DIM], BF16)
    bw_bf = SB("bw_bf", [128, KI * NST], BF16)
    cw_bf = SB("cw_bf", [128, KI * NST], BF16)
    acol = SB("acol", [128, 16 * NST], F32)
    convw = SB("convw_s", [128, 16 * 4], F32)
    convb = SB("convb_s", [128, 16], F32)
    dtbias = SB("dtbias", [128, 16], F32)
    eye_sb = SB("eye_sb", [128, 128], F32)
    ones1f = SB("ones1f_s", [1, 128], F32)
    onescol = SB("onescol_s", [128, 1], BF16)
    onesrow = SB("onesrow_s", [1, TC], BF16)
    stats = stats_t[:, :]
    xbf = lncast[:, 0:KD * TC]
    xsq = lncast[:, KD * TC:2 * KD * TC]
    wbuf2 = xs[:, 0:KI * TC // 2].bitcast(BF16)
    hbuf2 = xs[:, KI * TC // 2:KI * TC].bitcast(BF16)
    wbufs = [wbuf, wbuf2]
    hbufs = [hbuf, hbuf2]
    yprs = [ypr, ypr]
    yrows = [yrow, yrow2]
    cvals = SB("cvals_s", [128, 4], F32)
    zsm = SB("zsm", [128, 8], F32)
    bmcbf = SB("bmcbf", [NST, 2 * TC], BF16)
    rsets = []
    for si in range(2):
        rsets.append({
            "lgsb": SB(f"lgsb{si}", [NEXP, TC], F32),
            "ltm": SB(f"ltm{si}", [128, 16], F32),
            "exrow": SB(f"exrow{si}", [128, 16], F32),
            "mrow": SB(f"mrow{si}", [128, 16], F32),
            "msrow": SB(f"msrow{si}", [128, 16], F32),
            "mgtm": SB(f"mgtm{si}", [128, 16], F32),
            "rmax": SB(f"rmax{si}", [128, 2], F32),
            "nrm": SB(f"nrm{si}", [128, 2], F32),
            "ssum": SB(f"ssum{si}", [128, 2], F32),
            "g05": SB(f"g05{si}", [128, 2], F32),
            "mgbf": SB(f"mgbf{si}", [NEXP, TC], BF16),
            "mgrep": scr2[:, KD * TC:2 * KD * TC],
            "s_mg": None,
        })
    prep_f = [xs, xs]
    prep_b = [ypr, wbuf]

    rsets[0]["s_mg"] = s_mg0
    rsets[1]["s_mg"] = s_mg1
    extb = xe_sb[:, 0:KI * TC]
    xe_view = lambda slot, e: xe_sb[:, (slot * NEXP + e) * TC:(slot * NEXP + e + 1) * TC]

    pb = [PS(f"pb{i}") for i in range(4)]
    pstat = PS("pstat")
    prep_p = PS("prep_p")
    pbm = PS("pbm")
    pmisc = PS("pmisc")
    ffn_banks = pb + [pstat, prep_p, pbm, pmisc]

    nc.const_aps.register(F32, EPS, cvals[:, 0:1])
    nc.const_aps.register(F32, -0.5, cvals[:, 1:2])
    nc.const_aps.register(F32, 1.0, cvals[:, 2:3])
    nc.const_aps.register(F32, -1.0, cvals[:, 3:4])

    sems = {}
    for s in ["qs", "qc", "qg", "pe", "ve", "se", "od", "ge"]:
        sems[s] = stack.enter_context(nc.semaphore(s))

    prog = {k: [] for k in ["sync", "vector", "scalar", "gpsimd", "tensor"]}
    C = {k: 0 for k in sems}
    LW = {}

    def emit(eng, fn, inc=None, n=1):
        if inc is not None:
            C[inc] += n
            sem = sems[inc]
            prog[eng].append(lambda e, fn=fn, sem=sem, n=n: fn(e).then_inc(sem, n))
            return C[inc]
        prog[eng].append(fn)
        return None

    def wait(eng, sem_name, val):
        if val is None or val <= 0:
            return
        key = (eng, sem_name)
        if LW.get(key, -1) >= val:
            return
        LW[key] = val
        sem = sems[sem_name]
        prog[eng].append(lambda e, sem=sem, val=val: e.wait_ge(sem, val))

    QENG = {"qs": "sync", "qc": "scalar", "qg": "gpsimd"}

    def dma(q, out, in_, slow=False):
        C[q] += 16
        sem = sems[q]
        prog[QENG[q]].append(lambda e, out=out, in_=in_, sem=sem, slow=slow: e.dma_start(
            out=out, in_=in_, allow_slow_non_contiguous=slow).then_inc(sem, 16))
        return C[q]

    def vdrain():
        prog["vector"].append(lambda e: e.drain())

    small_loads = []

    def ld_small(out_ap, in_ap, slow=False):
        small_loads.append(dma("qs", out_ap, in_ap, slow=slow))

    ld_small(eye_sb[:, :], d_eye)
    ld_small(ones1f[:, :], d_ones1f)
    ld_small(onescol[:, :], d_onescol)
    ld_small(onesrow[:, :], d_onesrow)
    ld_small(cvals[:, :], d_cvals)
    stage_f = prep_f[0]
    ld_small(stage_f[0:1, 0:NEXP], d_rb)
    ld_small(stage_f[0:NEXP, 1024:1024 + DIM], d_eb)
    ld_small(stage_f[:, 2048:2048 + KD * NEXP].rearrange("p (kt e) -> p kt e", kt=KD),
             d_rw.rearrange("(kt p) e -> p kt e", p=128))
    ld_small(stage_f[:, 2112:2112 + KI * NST].rearrange("p (kt n) -> p kt n", kt=KI),
             d_BwT.rearrange("(kt p) n -> p kt n", p=128))
    ld_small(stage_f[:, 2368:2368 + KI * NST].rearrange("p (kt n) -> p kt n", kt=KI),
             d_CwT.rearrange("(kt p) n -> p kt n", p=128))
    ld_small(stage_f[:, 2624:2624 + 16 * NST].rearrange("p (dt n) -> p dt n", dt=16),
             d_alog.rearrange("(dt p) n -> p dt n", p=128))
    ld_small(convw[:, :].rearrange("p (dt k) -> p dt k", dt=16),
             d_convw.rearrange("(dt p) k -> p dt k", p=128))
    ld_small(convb[:, :], d_convb.rearrange("(dt p) -> p dt", p=128), slow=True)
    ld_small(dtbias[:, :], d_dtb.rearrange("(dt p) -> p dt", p=128), slow=True)
    smalls_loaded = C["qs"]

    wait("vector", "qs", smalls_loaded)
    emit("vector", lambda e: e.tensor_copy(out=rb_bf[:, :], in_=stage_f[0:1, 0:NEXP]))
    emit("vector", lambda e: e.tensor_copy(out=eb_bf[:, :], in_=stage_f[0:NEXP, 1024:1024 + DIM]))
    emit("vector", lambda e: e.tensor_copy(out=rw_bf[:, :], in_=stage_f[:, 2048:2048 + KD * NEXP]))
    emit("vector", lambda e: e.tensor_copy(out=bw_bf[:, :], in_=stage_f[:, 2112:2112 + KI * NST]))
    emit("vector", lambda e: e.tensor_copy(out=cw_bf[:, :], in_=stage_f[:, 2368:2368 + KI * NST]))
    emit("vector", lambda e: e.memset(zsm[:, :], 0.0))
    emit("vector", lambda e: e.memset(state[:, :], 0.0))
    emit("vector", lambda e: e.memset(carry[:, :], 0.0))
    smalls_cast = emit("vector", lambda e: e.memset(yrow[:, :], 0.0), inc="ve")
    wait("scalar", "qs", smalls_loaded)
    acol_exp = emit("scalar", lambda e: e.activation(out=acol[:, :], in_=stage_f[:, 2624:2624 + 16 * NST], func=AF.Exp), inc="se")
    wait("vector", "se", acol_exp)
    acol_done = emit("vector", lambda e: e.tensor_scalar(out=acol[:, :], in0=acol[:, :], scalar1=-1.0, scalar2=None, op0=ALU.mult), inc="ve")

    prep_chunks = []
    for k in range(KD):
        src = d_ipjT[k * 128:(k + 1) * 128, :]
        writes = [((m * 128, (m + 1) * 128), s_ipj[m, :, k * 128:(k + 1) * 128]) for m in range(32)]
        prep_chunks.append((src, 4096, writes))
    for k in range(KI):
        src = d_dtwT[k * 128:(k + 1) * 128, :]
        writes = [((m * 128, (m + 1) * 128), s_dtw[m, :, k * 128:(k + 1) * 128]) for m in range(16)]
        prep_chunks.append((src, 2048, writes))
    for k in range(KI):
        src = d_outwT[k * 128:(k + 1) * 128, :]
        writes = [((0, 1024), s_outw[k, :, :])]
        prep_chunks.append((src, 1024, writes))
    for e in range(NEXP):
        for q in range(2):
            src = d_ew[e, 512 * q:512 * (q + 1), :].rearrange("(kt p) c -> p kt c", p=128)
            writes = [((i * 1024, (i + 1) * 1024), s_exp[e, 4 * q + i, :, :]) for i in range(4)]
            prep_chunks.append((src, 4096, writes))

    wait("sync", "ve", max(smalls_cast, acol_done))
    wait("sync", "se", acol_exp)
    slot_store_done = [0, 0]
    prev_cast = 0
    for i, (src, fd, writes) in enumerate(prep_chunks):
        sl = i % 2
        wait("sync", "ve", prev_cast)
        wait("sync", "qc", slot_store_done[sl])
        dst = prep_f[0][:, 0:fd]
        if len(src.shape) == 3:
            dst = dst.rearrange("p (a b) -> p a b", a=src.shape[1])
        ldv = dma("qs", dst, src)
        wait("vector", "qs", ldv)
        cv = emit("vector", lambda e, sl=sl, fd=fd: e.tensor_copy(out=prep_b[sl][:, 0:fd], in_=prep_f[0][:, 0:fd]), inc="ve")
        prev_cast = cv
        wait("scalar", "ve", cv)
        for (c0, c1), dst in writes:
            slot_store_done[sl] = dma("qc", dst, prep_b[sl][:, c0:c1])
    PREP_QC = C["qc"]
    PREP_VE = C["ve"]

    bank_ready_pe = [0] * 4
    bank_free = [("ve", 0)] * 4
    st_free = {"pstat": ("ve", 0), "prep": ("ve", 0), "pbm": ("ve", 0), "pmisc": ("ve", 0)}
    wslot_free = {"ipj": [("pe", 0)] * 2, "dtw": [("pe", 0)] * 3, "outw": [("pe", 0)] * 3,
                  "exp": [("pe", 0)] * 3}
    wslot_loaded = {"ipj": [0] * 2, "dtw": [0] * 3, "outw": [0] * 3, "exp": [0] * 3}
    wcount = {"ipj": 0, "dtw": 0, "outw": 0, "exp": 0}
    WQ = {"ipj": "qc", "dtw": "qg", "outw": "qs", "exp": "qs"}
    WBUFS = {"ipj": ipj_w, "dtw": dtw_w, "outw": outw_w, "exp": exp_w}

    first_stream_wait = {"qc": PREP_QC, "qs": PREP_QC}

    def stream_weight(kind, src_ap):
        i = wcount[kind]
        wcount[kind] += 1
        bufs = WBUFS[kind]
        sl = i % len(bufs)
        q = WQ[kind]
        qe = QENG[q]
        wait(qe, "qc", PREP_QC)
        eng_s, val = wslot_free[kind][sl]
        wait(qe, eng_s, val)
        v = dma(q, bufs[sl][:, :src_ap.shape[-1] if False else bufs[sl].shape[1]], src_ap)
        wslot_loaded[kind][sl] = v
        return bufs[sl], v, sl

    def mark_wslot_consumed(kind, sl, pe_val):
        wslot_free[kind][sl] = ("pe", pe_val)

    prev_oop15 = 0
    prev_mask_done = 0
    prev_store = 0


    NSLOT = 26
    ST = [dict() for _ in range(NSTAGE + 1)]

    def Xb(k):
        return residR[(2 * k) % 4]

    def Ab(k):
        return residR[(2 * k + 1) % 4]

    def run_slots(sched, lo, hi):
        for s in range(lo, min(hi, NSLOT)):
            for t in sched[s]:
                t()
            sched[s] = []

    def ln_units(k, src_fn, xlnb, sched, s0, tag, cast_guard_fn, out, cb=None, apply_guard_fn=None):
        xbf_, xsq_ = cb if cb is not None else (xbf, xsq)
        def u_cast():
            xbf, xsq = xbf_, xsq_
            src = src_fn()
            for sname, val in cast_guard_fn():
                wait("scalar", sname, val)
            emit("scalar", lambda e, src=src: e.copy(out=xbf, in_=src[:, :]))
            v_sq = emit("scalar", lambda e, src=src: e.activation(out=xsq, in_=src[:, :], func=AF.Square), inc="se")
            wait("tensor", "se", v_sq)
            e_s, v_s = st_free["pstat"]
            wait("tensor", e_s, v_s)
            for kk in range(KD):
                emit("tensor", lambda e, kk=kk, xbf=xbf: e.matmul(out=pstat[0:1, 0:TC], lhsT=onescol[:, :],
                                                                  rhs=xbf[:, kk * TC:(kk + 1) * TC],
                                                                  start=(kk == 0), stop=(kk == KD - 1)))
            for kk in range(KD):
                emit("tensor", lambda e, kk=kk, xsq=xsq: e.matmul(out=pstat[0:1, TC:2 * TC], lhsT=onescol[:, :],
                                                                  rhs=xsq[:, kk * TC:(kk + 1) * TC],
                                                                  start=(kk == 0), stop=(kk == KD - 1)),
                     inc=("pe" if kk == KD - 1 else None))
            out["v_stats"] = C["pe"]

        def u_math():
            wait("vector", "pe", out["v_stats"])
            emit("vector", lambda e: e.tensor_scalar(out=stats[0:1, 0:TC], in0=pstat[0:1, 0:TC],
                                                     scalar1=1.0 / DIM, scalar2=None, op0=ALU.mult))
            vb4 = emit("vector", lambda e: e.tensor_scalar(out=stats[0:1, 512:512 + TC], in0=pstat[0:1, TC:2 * TC],
                                                           scalar1=1.0 / DIM, scalar2=None, op0=ALU.mult), inc="ve")
            st_free["pstat"] = ("ve", vb4)
            vdrain()
            emit("vector", lambda e: e.tensor_tensor(out=stats[0:1, 768:768 + TC], in0=stats[0:1, 0:TC],
                                                     in1=stats[0:1, 0:TC], op=ALU.mult))
            vdrain()
            emit("vector", lambda e: e.tensor_tensor(out=stats[0:1, 512:512 + TC], in0=stats[0:1, 512:512 + TC],
                                                     in1=stats[0:1, 768:768 + TC], op=ALU.subtract))
            vdrain()
            v_var = emit("vector", lambda e: e.tensor_scalar(out=stats[0:1, 512:512 + TC], in0=stats[0:1, 512:512 + TC],
                                                             scalar1=0.0, scalar2=None, op0=ALU.max), inc="ve")
            wait("scalar", "ve", v_var)
            emit("scalar", lambda e: e.activation(out=stats[0:1, 256:256 + TC], in_=stats[0:1, 512:512 + TC],
                                                  func=AF.Ln, bias=EPS))
            v_rstd = emit("scalar", lambda e: e.activation(out=stats[0:1, 256:256 + TC], in_=stats[0:1, 256:256 + TC],
                                                           func=AF.Exp, scale=-0.5), inc="se")
            wait("tensor", "se", v_rstd)
            e_s, v_s = st_free["prep"]
            wait("tensor", e_s, v_s)
            emit("tensor", lambda e: e.matmul(out=prep_p[:, 0:TC], lhsT=ones1f[:, :], rhs=stats[0:1, 0:TC],
                                              start=True, stop=True))
            emit("tensor", lambda e: e.matmul(out=prep_p[:, TC:2 * TC], lhsT=ones1f[:, :],
                                              rhs=stats[0:1, 256:256 + TC], start=True, stop=True), inc="pe")
            out["v_reps"] = C["pe"]

        def u_apply():
            src = src_fn()
            wait("vector", "pe", out["v_reps"])
            if apply_guard_fn is not None:
                for sname, val in apply_guard_fn():
                    wait("vector", sname, val)
            for kk in range(KD):
                emit("vector", lambda e, kk=kk, src=src, xsq=xsq_: e.tensor_tensor(
                    out=xsq[:, kk * TC:(kk + 1) * TC], in0=src[:, kk * TC:(kk + 1) * TC],
                    in1=prep_p[:, 0:TC], op=ALU.subtract))
            vdrain()
            for kk in range(KD):
                emit("vector", lambda e, kk=kk, xsq=xsq_: e.tensor_tensor(
                    out=xlnb[:, kk * TC:(kk + 1) * TC], in0=xsq[:, kk * TC:(kk + 1) * TC],
                    in1=prep_p[:, TC:2 * TC], op=ALU.mult),
                     inc=("ve" if kk == KD - 1 else None))
            out["v_xln"] = C["ve"]
            st_free["prep"] = ("ve", out["v_xln"])

        sched[s0].append(u_cast)
        sched[s0 + 1].append(u_math)
        sched[s0 + 2].append(u_apply)

    def router_units(k, xlnb, rset, sched, s0, out):
        R = rsets[rset]

        def u_logits():
            e_s, v_s = st_free["pmisc"]
            wait("tensor", e_s, v_s)
            wait("tensor", "ve", out["v_xln"])
            for kk in range(KD):
                emit("tensor", lambda e, kk=kk: e.matmul(out=pmisc[0:NEXP, 0:TC],
                                                         lhsT=rw_bf[:, kk * NEXP:(kk + 1) * NEXP],
                                                         rhs=xlnb[:, kk * TC:(kk + 1) * TC],
                                                         start=(kk == 0), stop=False))
            v_lg = emit("tensor", lambda e: e.matmul(out=pmisc[0:NEXP, 0:TC], lhsT=rb_bf[:, :],
                                                     rhs=onesrow[:, :], start=False, stop=True), inc="pe")
            wait("vector", "pe", v_lg)
            v_b7 = emit("vector", lambda e: e.tensor_copy(out=R["lgsb"][:, :], in_=pmisc[0:NEXP, 0:TC]), inc="ve")
            e_s, v_s = st_free["pstat"]
            wait("tensor", e_s, v_s)
            wait("tensor", "ve", v_b7)
            emit("tensor", lambda e: e.transpose(out=pstat[:, 0:8], in_=R["lgsb"][0:NEXP, 0:128],
                                                 identity=eye_sb[0:8, 0:8]))
            v_ltm = emit("tensor", lambda e: e.transpose(out=pstat[:, 8:16], in_=R["lgsb"][0:NEXP, 128:256],
                                                         identity=eye_sb[0:8, 0:8]), inc="pe")
            out["v_ltm"] = v_ltm
            out["v_b7"] = v_b7

        def u_soft():
            R_ = R
            wait("vector", "pe", out["v_ltm"])
            v_evac = emit("vector", lambda e: e.tensor_copy(out=R_["ltm"][:, :], in_=pstat[:, 0:16]), inc="ve")
            st_free["pstat"] = ("ve", v_evac)
            vdrain()
            for c in range(2):
                sl = slice(c * 8, (c + 1) * 8)
                emit("vector", lambda e, sl=sl, c=c: e.reduce_max(R_["rmax"][:, c:c + 1], R_["ltm"][:, sl], axis=AX.X))
            vdrain()
            for c in range(2):
                emit("vector", lambda e, c=c: e.tensor_scalar(out=R_["nrm"][:, c:c + 1], in0=R_["rmax"][:, c:c + 1],
                                                              scalar1=-1.0, scalar2=None, op0=ALU.mult),
                     inc=("ve" if c == 1 else None))
            v_nrm = C["ve"]
            wait("scalar", "ve", v_nrm)
            for c in range(2):
                sl = slice(c * 8, (c + 1) * 8)
                emit("scalar", lambda e, sl=sl, c=c: e.activation(out=R_["exrow"][:, sl], in_=R_["ltm"][:, sl],
                                                                  func=AF.Exp, bias=R_["nrm"][:, c:c + 1]),
                     inc=("se" if c == 1 else None))
            out["v_ex"] = C["se"]

        def u_mask():
            R_ = R
            wait("vector", "se", out["v_ex"])
            for c in range(2):
                sl = slice(c * 8, (c + 1) * 8)
                emit("vector", lambda e, sl=sl, c=c: e.reduce_sum(R_["ssum"][:, c:c + 1], R_["exrow"][:, sl], axis=AX.X))
            vdrain()
            for c in range(2):
                emit("vector", lambda e, c=c: e.reciprocal(R_["g05"][:, c:c + 1], R_["ssum"][:, c:c + 1]))
            vdrain()
            for c in range(2):
                emit("vector", lambda e, c=c: e.tensor_scalar(out=R_["g05"][:, c:c + 1], in0=R_["g05"][:, c:c + 1],
                                                              scalar1=0.5, scalar2=None, op0=ALU.mult))
            for c in range(2):
                sl = slice(c * 8, (c + 1) * 8)
                emit("vector", lambda e, sl=sl, c=c: e.tensor_scalar(out=R_["mrow"][:, sl], in0=R_["ltm"][:, sl],
                                                                     scalar1=R_["rmax"][:, c:c + 1], scalar2=None,
                                                                     op0=ALU.is_equal))
            vdrain()
            for c in range(2):
                sl = slice(c * 8, (c + 1) * 8)
                emit("vector", lambda e, sl=sl: e.tensor_tensor_scan(out=R_["msrow"][:, sl], data0=R_["mrow"][:, sl],
                                                                     data1=zsm[:, :], initial=0.0,
                                                                     op0=ALU.add, op1=ALU.add))
            vdrain()
            for c in range(2):
                sl = slice(c * 8, (c + 1) * 8)
                emit("vector", lambda e, sl=sl: e.tensor_scalar(out=R_["msrow"][:, sl], in0=R_["msrow"][:, sl],
                                                                scalar1=1.0, scalar2=None, op0=ALU.is_equal))
            vdrain()
            for c in range(2):
                sl = slice(c * 8, (c + 1) * 8)
                emit("vector", lambda e, sl=sl: e.tensor_tensor(out=R_["mrow"][:, sl], in0=R_["mrow"][:, sl],
                                                                in1=R_["msrow"][:, sl], op=ALU.mult))
            vdrain()
            for c in range(2):
                sl = slice(c * 8, (c + 1) * 8)
                emit("vector", lambda e, sl=sl, c=c: e.tensor_scalar(out=R_["mgtm"][:, sl], in0=R_["mrow"][:, sl],
                                                                     scalar1=R_["g05"][:, c:c + 1], scalar2=None,
                                                                     op0=ALU.mult), inc=("ve" if c == 1 else None))
            v_mgtm = C["ve"]
            e_s, v_s = st_free["pmisc"]
            wait("tensor", e_s, v_s)
            wait("tensor", "ve", v_mgtm)
            for c in range(2):
                emit("tensor", lambda e, c=c: e.transpose(out=pmisc[0:8, c * 128:(c + 1) * 128],
                                                          in_=R_["mgtm"][:, c * 8:(c + 1) * 8],
                                                          identity=eye_sb[:, :]), inc=("pe" if c == 1 else None))
            v_mgfm = C["pe"]
            wait("vector", "pe", v_mgfm)
            v_mgbf = emit("vector", lambda e: e.tensor_copy(out=R_["mgbf"][:, :], in_=pmisc[0:8, 0:TC]), inc="ve")
            st_free["pmisc"] = ("ve", v_mgbf)
            out["v_mgbf"] = v_mgbf
            wait("gpsimd", "ve", v_mgbf)
            v_w = dma("qg", R_["s_mg"], R_["mgbf"][:, :])
            wait("gpsimd", "qg", v_w)
            wait("gpsimd", "ve", mgrep_free[0])
            rep_src = R_["s_mg"].rearrange("e t -> (e t)").unsqueeze(0).broadcast_to((128, NEXP * TC))
            out["v_mgrep"] = dma("qg", R_["mgrep"][:, :], rep_src)

        sched[s0].append(u_logits)
        sched[s0 + 1].append(u_soft)
        sched[s0 + 2].append(u_mask)

    xe_last_pe = [0, 0]
    mgrep_free = [0]

    def ffn_units(k, xlnb, rset, src_fn, dst_fn, sched, s_mask0, s_evac, out):
        R = rsets[rset]
        xe_cons = xe_last_pe

        def mk_group(kk):
            def u_group():
                wait("vector", "qg", out["v_mgrep"])
                wait("vector", "pe", xe_cons[kk % 2])
                wait("vector", "ve", out["v_xln"])
                for e_i in range(NEXP):
                    emit("vector", lambda e, kk=kk, e_i=e_i: e.tensor_tensor(
                        out=xe_view(kk % 2, e_i), in0=xlnb[:, kk * TC:(kk + 1) * TC],
                        in1=R["mgrep"][:, e_i * TC:(e_i + 1) * TC], op=ALU.mult),
                        inc=("ve" if e_i == NEXP - 1 else None))
                xe_done = C["ve"]
                wait("tensor", "ve", xe_done)
                if kk == 0:
                    for b in range(4):
                        e_s, v_s = bank_free[b]
                        wait("tensor", e_s, v_s)
                    for key in ("pstat", "prep", "pbm", "pmisc"):
                        e_s, v_s = st_free[key]
                        wait("tensor", e_s, v_s)
                    wait("tensor", "ve", out["v_mgbf"])
                    for m in range(8):
                        emit("tensor", lambda e, m=m: e.matmul(out=ffn_banks[m][:, 0:TC],
                                                               lhsT=eb_bf[:, m * 128:(m + 1) * 128],
                                                               rhs=R["mgbf"][:, :], start=True, stop=False))
                for e_i in range(NEXP):
                    buf, ldv, sl = stream_weight("exp", s_exp[e_i, kk, :, :])
                    wait("tensor", WQ["exp"], ldv)
                    for m in range(8):
                        lastmm = (kk == KD - 1) and (e_i == NEXP - 1)
                        emit("tensor", lambda e, m=m, buf=buf, lastmm=lastmm, kk=kk, e_i=e_i: e.matmul(
                            out=ffn_banks[m][:, 0:TC], lhsT=buf[:, m * 128:(m + 1) * 128],
                            rhs=xe_view(kk % 2, e_i), start=False, stop=lastmm,
                            skip_group_check=True),
                            inc=("pe" if m == 7 else None))
                    mark_wslot_consumed("exp", sl, C["pe"])
                xe_cons[kk % 2] = C["pe"]
                if kk == KD - 1:
                    out["v_ffn_pe"] = C["pe"]
                    mgrep_free[0] = xe_done
            return u_group

        for kk in range(KD):
            sched[s_mask0 + kk].append(mk_group(kk))

        def u_evac():
            src = src_fn()
            dst = dst_fn()
            wait("vector", "pe", out["v_ffn_pe"])
            if rset == 0 and k >= 2:
                wait("vector", "od", ST[k - 2]["v_store"])
            for m in range(8):
                emit("vector", lambda e, m=m, src=src, dst=dst: e.scalar_tensor_tensor(
                    out=dst[:, m * TC:(m + 1) * TC], in0=ffn_banks[m][:, 0:TC], scalar=1.0,
                    in1=src[:, m * TC:(m + 1) * TC], op0=ALU.mult, op1=ALU.add),
                    inc=("ve" if m == 7 else None))
            out["v_evac"] = C["ve"]
            for b in range(4):
                bank_free[b] = ("ve", out["v_evac"])
            for key in ("pstat", "prep", "pbm", "pmisc"):
                st_free[key] = ("ve", out["v_evac"])

        sched[s_evac].append(u_evac)

    def plan_front(k, sched):
        st = ST[k]
        st["ln1"] = {}
        st["r1"] = {}
        st["f1"] = {}
        st["ipx"] = [0] * 16
        st["sres"] = [0] * 16
        s = k // (L // TC)
        c = k % (L // TC)
        st["s"], st["c"] = s, c
        x_src = d_x[s, :, c * TC:(c + 1) * TC].rearrange("(dp p) t -> p dp t", p=128)

        def u_xload():
            if k >= 2:
                wait("sync", "ve", ST[k - 2]["f2"]["v_evac"])
                wait("sync", "od", ST[k - 2]["v_store"])
            st["v_x"] = dma("qs", Xb(k)[:, :].rearrange("p (dp t) -> p dp t", dp=KD), x_src)
        sched[2].append(u_xload)

        def cast_guard1():
            return [("qs", st["v_x"])]

        def apply_guard1():
            if k >= 1:
                return [("pe", ST[k - 1]["ipj_pe"])]
            return []
        ln_units(k, lambda: Xb(k), xln0, sched, 3, "ln1", cast_guard1, st["ln1"],
                 apply_guard_fn=apply_guard1)
        st["r1"]["v_xln"] = None

        def link_r1():
            st["r1"]["v_xln"] = st["ln1"]["v_xln"]
        sched[5].append(link_r1)
        router_units(k, xln0, 0, sched, 5, st["r1"])
        st["f1"]["v_mgrep"] = None

        def link_f1():
            st["f1"]["v_mgrep"] = st["r1"]["v_mgrep"]
            st["f1"]["v_mgbf"] = st["r1"]["v_mgbf"]
            st["f1"]["v_xln"] = st["ln1"]["v_xln"]
        sched[7].append(link_f1)
        ffn_units(k, xln0, 0, lambda: Xb(k), lambda: Ab(k), sched, 8, 16, st["f1"])

        st["ln2"] = {}

        def cast_guard2():
            return [("ve", st["f1"]["v_evac"])]

        if DEBUG_DUMP:
            def u_dbg1():
                dbg1_dst = d_dbg1[s, :, c * TC:(c + 1) * TC].rearrange("(dp p) t -> p dp t", p=128)
                wait("sync", "ve", st["f1"]["v_evac"])
                C["od"] += 16
                prog["sync"].append(lambda e, dbg1_dst=dbg1_dst: e.dma_start(
                    out=dbg1_dst, in_=Ab(k)[:, :].rearrange("p (dp t) -> p dp t", dp=KD)).then_inc(sems["od"], 16))
                C["od"] += 16
                prog["sync"].append(lambda e, k=k: e.dma_start(
                    out=d_dbg3[k], in_=rsets[0]["mgbf"][:, :]).then_inc(sems["od"], 16))
                C["od"] += 16
                prog["sync"].append(lambda e, k=k: e.dma_start(
                    out=d_dbg4[k], in_=rsets[0]["mgrep"][0:1, :]).then_inc(sems["od"], 16))
            sched[17].append(u_dbg1)
        ln_units(k, lambda: Ab(k), xln0, sched, 16, "ln2", cast_guard2, st["ln2"])

        def u_ipj():
            for m in range(32):
                buf, ldv, sl = stream_weight("ipj", s_ipj[m, :, :])
                b = m % 4
                e_s, v_s = bank_free[b]
                wait("tensor", e_s, v_s)
                wait("tensor", WQ["ipj"], ldv)
                wait("tensor", "ve", st["ln2"]["v_xln"])
                for kk in range(KD):
                    emit("tensor", lambda e, b=b, kk=kk, buf=buf: e.matmul(
                        out=pb[b][:, 0:TC], lhsT=buf[:, kk * 128:(kk + 1) * 128],
                        rhs=xln0[:, kk * TC:(kk + 1) * TC], start=(kk == 0), stop=(kk == KD - 1)),
                        inc=("pe" if kk == KD - 1 else None))
                mark_wslot_consumed("ipj", sl, C["pe"])
                v_pe = C["pe"]
                if m < 16:
                    wait("scalar", "pe", v_pe)
                    v_e = emit("scalar", lambda e, m=m, b=b: e.copy(
                        out=extb[:, m * TC:(m + 1) * TC], in_=pb[b][:, 0:TC]), inc="se")
                    st["ipx"][m] = v_e
                    bank_free[b] = ("se", v_e)
                else:
                    mm = m - 16
                    wait("scalar", "pe", v_pe)
                    if k >= 1:
                        wait("scalar", "ve", ST[k - 1]["oop_done"][mm])
                    v_e = emit("scalar", lambda e, mm=mm, b=b: e.activation(
                        out=sres[:, mm * TC:(mm + 1) * TC], in_=pb[b][:, 0:TC], func=AF.Silu), inc="se")
                    st["sres"][mm] = v_e
                    bank_free[b] = ("se", v_e)
            st["ipj_pe"] = C["pe"]
        sched[18].append(u_ipj)

        def mk_conv(level):
            def u_conv():
                nc_ = 3 - level
                if level == 0:
                    wait("vector", "se", max(st["ipx"]))
                    vdrain()
                    if c == 0:
                        emit("vector", lambda e: e.memset(carry[:, :], 0.0))
                        vdrain()
                for j in range(16):
                    if nc_ > 0:
                        if level == 0:
                            emit("vector", lambda e, j=j, nc_=nc_: e.tensor_scalar(
                                out=xs[:, j * TC:j * TC + nc_], in0=carry[:, j * 4 + level:j * 4 + 3],
                                scalar1=convw[:, j * 4 + level:j * 4 + level + 1],
                                scalar2=convb[:, j:j + 1], op0=ALU.mult, op1=ALU.add))
                        else:
                            emit("vector", lambda e, j=j, nc_=nc_, level=level: e.scalar_tensor_tensor(
                                out=xs[:, j * TC:j * TC + nc_], in0=carry[:, j * 4 + level:j * 4 + 3],
                                scalar=convw[:, j * 4 + level:j * 4 + level + 1],
                                in1=xs[:, j * TC:j * TC + nc_], op0=ALU.mult, op1=ALU.add))
                    if level == 0:
                        emit("vector", lambda e, j=j, nc_=nc_: e.tensor_scalar(
                            out=xs[:, j * TC + nc_:(j + 1) * TC], in0=extb[:, j * TC:(j + 1) * TC - nc_],
                            scalar1=convw[:, j * 4:j * 4 + 1], scalar2=convb[:, j:j + 1],
                            op0=ALU.mult, op1=ALU.add))
                    else:
                        emit("vector", lambda e, j=j, nc_=nc_, level=level: e.scalar_tensor_tensor(
                            out=xs[:, j * TC + nc_:(j + 1) * TC], in0=extb[:, j * TC:(j + 1) * TC - nc_],
                            scalar=convw[:, j * 4 + level:j * 4 + level + 1],
                            in1=xs[:, j * TC + nc_:(j + 1) * TC], op0=ALU.mult, op1=ALU.add))
                vdrain()
                if level == 3:
                    for j in range(16):
                        emit("vector", lambda e, j=j: e.tensor_copy(out=carry[:, j * 4:j * 4 + 3],
                                                                    in_=extb[:, (j + 1) * TC - 3:(j + 1) * TC]),
                             inc=("ve" if j == 15 else None))
                    st["v_conv"] = C["ve"]
            return u_conv
        for lv in range(4):
            sched[19 + lv].append(mk_conv(lv))

        def u_silu():
            wait("scalar", "ve", st["v_conv"])
            for j in range(16):
                emit("scalar", lambda e, j=j: e.activation(out=xsbf[:, j * TC:(j + 1) * TC],
                                                           in_=xs[:, j * TC:(j + 1) * TC], func=AF.Silu),
                     inc=("se" if j == 15 else None))
            st["v_silu"] = C["se"]
        sched[23].append(u_silu)

    def plan_mid(k, sched):
        st = ST[k]
        st["dt_done"] = [0] * 16
        st["u_done"] = [0] * 16

        def mk_dt_group(ms):
            def u_dtg():
                pes = {}
                wait("tensor", "se", st["v_silu"])
                for m in ms:
                    bufA, ldvA, slA = stream_weight("dtw", s_dtw[m, :, 0:KD * 128])
                    bufB, ldvB, slB = stream_weight("dtw", s_dtw[m, :, KD * 128:KI * 128])
                    b = m % 4
                    e_s, v_s = bank_free[b]
                    wait("tensor", e_s, v_s)
                    wait("tensor", WQ["dtw"], max(ldvA, ldvB))
                    for kk in range(KI):
                        buf = bufA if kk < KD else bufB
                        emit("tensor", lambda e, b=b, kk=kk, buf=buf: e.matmul(
                            out=pb[b][:, 0:TC], lhsT=buf[:, (kk % KD) * 128:(kk % KD + 1) * 128],
                            rhs=xsbf[:, kk * TC:(kk + 1) * TC], start=(kk == 0), stop=(kk == KI - 1)),
                            inc=("pe" if kk == KI - 1 else None))
                    mark_wslot_consumed("dtw", slA, C["pe"])
                    mark_wslot_consumed("dtw", slB, C["pe"])
                    pes[m] = C["pe"]
                for m in ms:
                    b = m % 4
                    sp = m % 4
                    spa = spscr[:, 0:TC] if sp == 0 else None
                    wait("scalar", "pe", pes[m])
                    if k >= 1:
                        wait("scalar", "ve", ST[k - 1]["eg_val"][m][NST - 1])
                    spa = spscr[:, (sp % 3) * 2 * TC:(sp % 3) * 2 * TC + TC]
                    spb = spscr[:, (sp % 3) * 2 * TC + TC:((sp % 3) + 1) * 2 * TC]
                    if m >= 3:
                        wait("scalar", "ve", st["dt_done"].get(m - 3, 0) if isinstance(st["dt_done"], dict) else st["dt_done"][m - 3])
                    emit("scalar", lambda e, b=b, spa=spa, m=m: e.activation(
                        out=spa, in_=pb[b][:, 0:TC], func=AF.Abs, bias=dtbias[:, m:m + 1]))
                    emit("scalar", lambda e, spa=spa: e.activation(out=spa, in_=spa, func=AF.Exp, scale=-1.0))
                    emit("scalar", lambda e, spa=spa: e.activation(out=spa, in_=spa, func=AF.Ln, bias=1.0))
                    v_sp = emit("scalar", lambda e, b=b, spb=spb, m=m: e.activation(
                        out=spb, in_=pb[b][:, 0:TC], func=AF.Relu, bias=dtbias[:, m:m + 1]), inc="se")
                    bank_free[b] = ("se", v_sp)
                    wait("vector", "se", v_sp)
                    v_dt = emit("vector", lambda e, m=m, spa=spa, spb=spb: e.tensor_tensor(
                        out=dtb_[:, m * TC:(m + 1) * TC], in0=spa, in1=spb, op=ALU.add), inc="ve")
                    st["dt_done"][m] = v_dt
                    v_u = emit("vector", lambda e, m=m: e.tensor_tensor(out=ubf[:, m * TC:(m + 1) * TC],
                                                                        in0=dtb_[:, m * TC:(m + 1) * TC],
                                                                        in1=xsbf[:, m * TC:(m + 1) * TC], op=ALU.mult),
                               inc="ve")
                    st["u_done"][m] = v_u
                vdrain()
            return u_dtg

        sched[24].append(mk_dt_group([0, 1, 2, 3]))
        sched[24].append(mk_dt_group([4, 5, 6, 7]))
        sched[25].append(mk_dt_group([8, 9, 10, 11]))
        sched2 = st.setdefault("own_sched_hooks", {})
        sched2[2] = mk_dt_group([12, 13, 14, 15])

        def u_bc():
            e_s, v_s = st_free["pbm"]
            wait("tensor", e_s, v_s)
            wait("tensor", "se", st["v_silu"])
            for kk in range(KI):
                emit("tensor", lambda e, kk=kk: e.matmul(out=pbm[0:NST, 0:TC],
                                                         lhsT=bw_bf[:, kk * NST:(kk + 1) * NST],
                                                         rhs=xsbf[:, kk * TC:(kk + 1) * TC],
                                                         start=(kk == 0), stop=(kk == KI - 1)))
            for kk in range(KI):
                emit("tensor", lambda e, kk=kk: e.matmul(out=pbm[0:NST, TC:2 * TC],
                                                         lhsT=cw_bf[:, kk * NST:(kk + 1) * NST],
                                                         rhs=xsbf[:, kk * TC:(kk + 1) * TC],
                                                         start=(kk == 0), stop=(kk == KI - 1)),
                     inc=("pe" if kk == KI - 1 else None))
            st["bc_pe"] = C["pe"]
            wait("vector", "pe", st["bc_pe"])
            v_bmc = emit("vector", lambda e: e.tensor_copy(out=bmcbf[:, :], in_=pbm[0:NST, 0:2 * TC]), inc="ve")
            st_free["pbm"] = ("ve", v_bmc)
            wait("gpsimd", "ve", v_bmc)
            v_wb = dma("qg", s_bmc, bmcbf[:, :])
            wait("gpsimd", "qg", v_wb)
            if k >= 1:
                wait("gpsimd", "ve", ST[k - 1]["wmul_last"])
                wait("gpsimd", "ve", ST[k - 1]["ymul_last"])
            st["v_brep"] = dma("qg", brep[:, :].rearrange("p (n t) -> p n t", n=NST),
                               s_bmc[:, 0:TC].unsqueeze(0).broadcast_to((128, NST, TC)))
            st["v_crep"] = dma("qg", crep[:, :].rearrange("p (n t) -> p n t", n=NST),
                               s_bmc[:, TC:2 * TC].unsqueeze(0).broadcast_to((128, NST, TC)))
        sched[25].append(u_bc)

    def emit_back(k, sched, sched_next):
        st = ST[k]
        s = st["s"]
        st["eg_val"] = [[0] * NST for _ in range(16)]
        scan_val = [[0] * NST for _ in range(16)]
        st["oop_done"] = [0] * 16
        wmul_val = [0] * 16
        ymul_val = [0] * 16
        red_val = [0] * 16
        soffs = [s * 256 + jj * NST for jj in range(16)]
        hooks2 = st.get("own_sched_hooks", {})

        def em_wmul(j):
            wait("vector", "ve", st["u_done"][j])
            wait("vector", "se", st["v_silu"])
            wait("vector", "qg", max(st["v_brep"], st["v_crep"]))
            u_b = ubf[:, j * TC:(j + 1) * TC].unsqueeze(1).broadcast_to((128, NST, TC))
            wb = wbufs[j % 2]
            wmul_val[j] = emit("vector", lambda e, u_b=u_b, wb=wb: e.tensor_tensor(
                out=wb[:, :].rearrange("p (n t) -> p n t", n=NST), in0=u_b,
                in1=brep[:, :].rearrange("p (n t) -> p n t", n=NST), op=ALU.mult), inc="ve")

        def em_scans(j):
            wait("vector", "ve", wmul_val[j])
            wait("vector", "se", st["v_silu"])
            wait("scalar", "ve", st["dt_done"][j])
            wb = wbufs[j % 2]
            hb = hbufs[j % 2]
            for n in range(NST):
                if n >= 3:
                    wait("scalar", "ve", scan_val[j][n - 3])
                elif j > 0:
                    wait("scalar", "ve", scan_val[j - 1][NST - 3 + n])
                sl4 = (n % 3) * TC
                st["eg_val"][j][n] = emit("scalar", lambda e, j=j, n=n, sl4=sl4: e.activation(
                    out=egb[:, sl4:sl4 + TC], in_=dtb_[:, j * TC:(j + 1) * TC],
                    func=AF.Exp, scale=acol[:, j * NST + n:j * NST + n + 1]), inc="se")
                wait("vector", "se", st["eg_val"][j][n])
                scan_val[j][n] = emit("vector", lambda e, n=n, j=j, wb=wb, hb=hb, sl4=sl4: e.tensor_tensor_scan(
                    out=hb[:, n * TC:(n + 1) * TC], data0=egb[:, sl4:sl4 + TC],
                    data1=wb[:, n * TC:(n + 1) * TC], initial=state[:, soffs[j] + n:soffs[j] + n + 1],
                    op0=ALU.mult, op1=ALU.add), inc="ve")

        def em_state_ymul(j):
            wait("vector", "ve", scan_val[j][NST - 1])
            hb = hbufs[j % 2]
            hview = hb[:, :].rearrange("p (n t) -> p n t", n=NST)
            emit("vector", lambda e, j=j, hview=hview: e.tensor_copy(
                out=state[:, soffs[j]:soffs[j] + NST], in_=hview[:, :, TC - 1:TC]))
            if j >= 1:
                wait("vector", "ge", red_val[j - 1])
            elif k >= 1:
                wait("vector", "ge", ST[k - 1]["red15_ge"])
            yp = yprs[j % 2]
            ymul_val[j] = emit("vector", lambda e, hview=hview, yp=yp: e.tensor_tensor(
                out=yp[:, 0:NST * TC].rearrange("p (n t) -> p n t", n=NST), in0=hview,
                in1=crep[:, :].rearrange("p (n t) -> p n t", n=NST), op=ALU.mult), inc="ve")

        def em_reduce(j):
            wait("gpsimd", "ve", ymul_val[j])
            if j >= 2:
                wait("gpsimd", "ve", st["oop_done"][j - 2])
            yp = yprs[j % 2]
            for width in (8, 4, 2):
                emit("gpsimd", lambda e, yp=yp, width=width: e.tensor_tensor(
                    out=yp[:, 0:width * TC], in0=yp[:, 0:width * TC],
                    in1=yp[:, width * TC:2 * width * TC], op=ALU.add))
            red_val[j] = emit("gpsimd", lambda e, yp=yp: e.tensor_tensor(
                out=yrows[j % 2][:, :], in0=yp[:, 0:TC], in1=yp[:, TC:2 * TC],
                op=ALU.add), inc="ge")

        def em_outop(j):
            wait("vector", "ge", red_val[j])
            wait("vector", "se", st["sres"][j])
            if j == 0 and k >= 1:
                wait("vector", "pe", ST[k - 1]["ln3_pe_guard"])
            st["oop_done"][j] = emit("vector", lambda e, j=j: e.tensor_tensor(
                out=oop[:, j * TC:(j + 1) * TC], in0=yrows[j % 2][:, :],
                in1=sres[:, j * TC:(j + 1) * TC], op=ALU.mult), inc="ve")

        vdrain()
        em_wmul(0)
        em_wmul(1)
        for j in range(19):
            if j < 16:
                em_scans(j)
            if 1 <= j <= 16:
                em_state_ymul(j - 1)
            if 2 <= j <= 17:
                em_reduce(j - 2)
            if 3 <= j <= 18:
                em_outop(j - 3)
            if j + 2 <= 15:
                em_wmul(j + 2)
            if j in hooks2:
                hooks2[j]()
            run_slots(sched, j, j + 1)
        st["wmul_last"] = wmul_val[15]
        st["ymul_last"] = ymul_val[15]
        st["red15_ge"] = red_val[15]
        run_slots(sched, 19, NSLOT)

        for b in range(4):
            e_s, v_s = bank_free[b]
            wait("tensor", e_s, v_s)
        for key in ("pstat", "prep", "pbm", "pmisc"):
            e_s, v_s = st_free[key]
            wait("tensor", e_s, v_s)
        for kk in range(KI):
            buf, ldv, sl = stream_weight("outw", s_outw[kk, :, :])
            wait("tensor", WQ["outw"], ldv)
            wait("tensor", "ve", st["oop_done"][kk])
            for m in range(8):
                emit("tensor", lambda e, kk=kk, m=m, buf=buf: e.matmul(
                    out=ffn_banks[m][:, 0:TC], lhsT=buf[:, m * 128:(m + 1) * 128],
                    rhs=oop[:, kk * TC:(kk + 1) * TC], start=(kk == 0), stop=(kk == KI - 1)),
                    inc=("pe" if m == 7 else None))
            mark_wslot_consumed("outw", sl, C["pe"])
        v_outmm = C["pe"]
        wait("vector", "pe", v_outmm)
        wait("vector", "ge", red_val[15])
        for m in range(8):
            emit("vector", lambda e, m=m: e.scalar_tensor_tensor(
                out=Xb(k)[:, m * TC:(m + 1) * TC], in0=ffn_banks[m][:, 0:TC], scalar=1.0,
                in1=Ab(k)[:, m * TC:(m + 1) * TC], op0=ALU.mult, op1=ALU.add), inc=("ve" if m == 7 else None))
        v_battn = C["ve"]
        for b in range(4):
            bank_free[b] = ("ve", v_battn)
        for key in ("pstat", "prep", "pbm", "pmisc"):
            st_free[key] = ("ve", v_battn)

        if DEBUG_DUMP:
            dbg2_dst = d_dbg2[s, :, st["c"] * TC:(st["c"] + 1) * TC].rearrange("(dp p) t -> p dp t", p=128)
            C["od"] += 16
            prog["sync"].append(lambda e, dbg2_dst=dbg2_dst, vb=v_battn: (
                e.wait_ge(sems["ve"], vb),
                e.dma_start(out=dbg2_dst, in_=Xb(k)[:, :].rearrange("p (dp t) -> p dp t", dp=KD)).then_inc(sems["od"], 16)))
        st["ln3"] = {}
        st["r2"] = {}
        st["f2"] = {}
        direct = [[] for _ in range(NSLOT)]

        def guard3():
            return [("ve", v_battn)]
        ln_units(k, lambda: Xb(k), xln1, direct, 0, "ln3", guard3, st["ln3"],
                 cb=(oop[:, 0:KD * TC], oop[:, KD * TC:2 * KD * TC]))
        run_slots(direct, 0, 3)
        st["ln3_pe_guard"] = C["pe"]

        def link_r2():
            st["r2"]["v_xln"] = st["ln3"]["v_xln"]
        direct[0].append(link_r2)
        router_units(k, xln1, 1, direct, 0, st["r2"])
        run_slots(direct, 0, 3)

        def link_f2():
            st["f2"]["v_mgrep"] = st["r2"]["v_mgrep"]
            st["f2"]["v_mgbf"] = st["r2"]["v_mgbf"]
            st["f2"]["v_xln"] = st["ln3"]["v_xln"]
        link_f2()
        ffn_units(k, xln1, 1, lambda: Xb(k), lambda: Ab(k), direct, 0, 8, st["f2"])
        run_slots(direct, 0, 8)

        def u_f2_evac_store():
            for t in direct[8]:
                t()
            out_dst = d_out[st["s"], :, st["c"] * TC:(st["c"] + 1) * TC].rearrange("(dp p) t -> p dp t", p=128)
            wait("sync", "ve", st["f2"]["v_evac"])
            C["od"] += 16
            prog["sync"].append(lambda e, out_dst=out_dst: e.dma_start(
                out=out_dst, in_=Ab(k)[:, :].rearrange("p (dp t) -> p dp t", dp=KD)).then_inc(sems["od"], 16))
            st["v_store"] = C["od"]
        sched_next[1].append(u_f2_evac_store)

    sched_cur = [[] for _ in range(NSLOT)]
    sched_next = [[] for _ in range(NSLOT)]
    for k in range(NSTAGE):
        plan_front(k, sched_cur)
        plan_mid(k, sched_cur)
        if k >= 1:
            emit_back(k - 1, sched_cur, sched_next)
        else:
            run_slots(sched_cur, 0, NSLOT)
            hooks2 = ST[0].get("own_sched_hooks", {})
            for j in sorted(hooks2):
                hooks2[j]()
            ST[0]["own_sched_hooks"] = {}
        sched_cur, sched_next = sched_next, [[] for _ in range(NSLOT)]
    emit_back(NSTAGE - 1, sched_cur, sched_next)
    run_slots(sched_next, 0, NSLOT)
    if DEBUG_DUMP:
        wait("vector", "od", C["od"])
        fin = emit("vector", lambda e: e.tensor_copy(out=xs[:, 0:1], in_=xs[:, 0:1]), inc="ve")
        wait("sync", "ve", fin)
        for idx_, src_ in [(0, sres), (1, crep), (2, brep), (3, oop)]:
            C["od"] += 16
            prog["sync"].append(lambda e, idx_=idx_, src_=src_: e.dma_start(
                out=d_dbg5[idx_], in_=src_[:, :]).then_inc(sems["od"], 16))

    wait("sync", "od", C["od"])

    blockctx = nc.Block()
    block = blockctx.__enter__()

    @block.sync
    def _(eng):
        for f in prog["sync"]:
            f(eng)

    @block.vector
    def _(eng):
        for f in prog["vector"]:
            f(eng)

    @block.scalar
    def _(eng):
        for f in prog["scalar"]:
            f(eng)

    @block.gpsimd
    def _(eng):
        for f in prog["gpsimd"]:
            f(eng)

    @block.tensor
    def _(eng):
        for f in prog["tensor"]:
            f(eng)

    blockctx.__exit__(None, None, None)
    stack.close()
    return nc


def prepare(L0_attn_x, L0_attn_in_proj_w, L0_attn_conv_w, L0_attn_conv_b,
            L0_attn_dt_w, L0_attn_dt_b, L0_attn_A_log, L0_attn_B_w, L0_attn_C_w,
            L0_attn_out_w, L0_ffn_router_w, L0_ffn_router_b, L0_ffn_expert_w,
            L0_ffn_expert_b):
    x = np.asarray(L0_attn_x, dtype=np.float32)
    nc = build_nc()
    common = {
        "ipjT": np.ascontiguousarray(np.asarray(L0_attn_in_proj_w, np.float32).T),
        "convw": np.ascontiguousarray(np.asarray(L0_attn_conv_w, np.float32)),
        "convb": np.ascontiguousarray(np.asarray(L0_attn_conv_b, np.float32)),
        "dtwT": np.ascontiguousarray(np.asarray(L0_attn_dt_w, np.float32).T),
        "dtb": np.ascontiguousarray(np.asarray(L0_attn_dt_b, np.float32)),
        "alog": np.ascontiguousarray(np.asarray(L0_attn_A_log, np.float32)),
        "BwT": np.ascontiguousarray(np.asarray(L0_attn_B_w, np.float32).T),
        "CwT": np.ascontiguousarray(np.asarray(L0_attn_C_w, np.float32).T),
        "outwT": np.ascontiguousarray(np.asarray(L0_attn_out_w, np.float32).T),
        "rw": np.ascontiguousarray(np.asarray(L0_ffn_router_w, np.float32)),
        "rb": np.ascontiguousarray(np.asarray(L0_ffn_router_b, np.float32).reshape(1, NEXP)),
        "ew": np.ascontiguousarray(np.asarray(L0_ffn_expert_w, np.float32)),
        "eb": np.ascontiguousarray(np.asarray(L0_ffn_expert_b, np.float32)),
        "eye": np.eye(128, dtype=np.float32),
        "ones1f": np.ones((1, 128), np.float32),
        "onescol": np.ones((128, 1), ml_dtypes.bfloat16),
        "onesrow": np.ones((1, TC), ml_dtypes.bfloat16),
        "cvals": np.tile(np.array([[EPS, -0.5, 1.0, -1.0]], np.float32), (128, 1)),
    }
    in_maps = []
    for i in range(NCORES):
        xi = np.ascontiguousarray(x[BPC * i:BPC * (i + 1)].transpose(0, 2, 1))
        m = dict(common)
        m["x"] = xi
        in_maps.append(m)
    return nc, in_maps


def kernel(**inputs):
    nc, in_maps = prepare(**inputs)
    res = run_bass_kernel_spmd(nc, in_maps, core_ids=list(range(NCORES)))
    outs = []
    for i in range(NCORES):
        o = res.results[i]["out"]
        outs.append(o.transpose(0, 2, 1))
    return np.ascontiguousarray(np.concatenate(outs, axis=0))


if __name__ == "__main__":
    print("building program...")
    nc = build_nc()
    print("built ok")


# version 21
# speedup vs baseline: 1.0246x; 1.0246x over previous
import numpy as np
import ml_dtypes
import concourse.bass as bass
import concourse.mybir as mybir
from concourse.bass_utils import run_bass_kernel_spmd
from contextlib import ExitStack

F32 = mybir.dt.float32
BF16 = mybir.dt.bfloat16
AF = mybir.ActivationFunctionType
ALU = mybir.AluOpType
AX = mybir.AxisListType

DEBUG_DUMP = False
USE_POOL_TREE = True
NCORES = 8
BPC = 2
L = 2048
DIM = 1024
INNER = 2048
NST = 16
NEXP = 8
TC = 256
NSTAGE = BPC * (L // TC)
KD = DIM // 128
KI = INNER // 128
EPS = 1e-5


def build_nc():
    nc = bass.Bass("TRN2", target_bir_lowering=False)

    d_x = nc.dram_tensor("x", [BPC, DIM, L], F32, kind="ExternalInput").ap()
    d_ipjT = nc.dram_tensor("ipjT", [DIM, 2 * INNER], F32, kind="ExternalInput").ap()
    d_convw = nc.dram_tensor("convw", [INNER, 4], F32, kind="ExternalInput").ap()
    d_convb = nc.dram_tensor("convb", [INNER], F32, kind="ExternalInput").ap()
    d_dtwT = nc.dram_tensor("dtwT", [INNER, INNER], F32, kind="ExternalInput").ap()
    d_dtb = nc.dram_tensor("dtb", [INNER], F32, kind="ExternalInput").ap()
    d_alog = nc.dram_tensor("alog", [INNER, NST], F32, kind="ExternalInput").ap()
    d_BwT = nc.dram_tensor("BwT", [INNER, NST], F32, kind="ExternalInput").ap()
    d_CwT = nc.dram_tensor("CwT", [INNER, NST], F32, kind="ExternalInput").ap()
    d_outwT = nc.dram_tensor("outwT", [INNER, DIM], F32, kind="ExternalInput").ap()
    d_rw = nc.dram_tensor("rw", [DIM, NEXP], F32, kind="ExternalInput").ap()
    d_rb = nc.dram_tensor("rb", [1, NEXP], F32, kind="ExternalInput").ap()
    d_ew = nc.dram_tensor("ew", [NEXP, DIM, DIM], F32, kind="ExternalInput").ap()
    d_eb = nc.dram_tensor("eb", [NEXP, DIM], F32, kind="ExternalInput").ap()
    d_eye = nc.dram_tensor("eye", [128, 128], F32, kind="ExternalInput").ap()
    d_ones1f = nc.dram_tensor("ones1f", [1, 128], F32, kind="ExternalInput").ap()
    d_onescol = nc.dram_tensor("onescol", [128, 1], BF16, kind="ExternalInput").ap()
    d_onesrow = nc.dram_tensor("onesrow", [1, TC], BF16, kind="ExternalInput").ap()
    d_cvals = nc.dram_tensor("cvals", [128, 4], F32, kind="ExternalInput").ap()

    d_out = nc.dram_tensor("out", [BPC, DIM, L], F32, kind="ExternalOutput").ap()
    if DEBUG_DUMP:
        d_dbg1 = nc.dram_tensor("dbg1", [BPC, DIM, L], F32, kind="ExternalOutput").ap()
        d_dbg2 = nc.dram_tensor("dbg2", [BPC, DIM, L], F32, kind="ExternalOutput").ap()
        d_dbg3 = nc.dram_tensor("dbg3", [NSTAGE, NEXP, TC], BF16, kind="ExternalOutput").ap()
        d_dbg4 = nc.dram_tensor("dbg4", [NSTAGE, 1, NEXP * TC], BF16, kind="ExternalOutput").ap()
        d_dbg5 = nc.dram_tensor("dbg5", [4, 128, KI * TC], BF16, kind="ExternalOutput").ap()

    s_ipj = nc.dram_tensor("s_ipj", [32, 128, KD * 128], BF16, kind="Internal").ap()
    s_dtw = nc.dram_tensor("s_dtw", [16, 128, KI * 128], BF16, kind="Internal").ap()
    s_outw = nc.dram_tensor("s_outw", [16, 128, KD * 128], BF16, kind="Internal").ap()
    s_exp = nc.dram_tensor("s_exp", [NEXP, KD, 128, DIM], BF16, kind="Internal").ap()
    s_mg0 = nc.dram_tensor("s_mg0", [NEXP, TC], BF16, kind="Internal").ap()
    s_mg1 = nc.dram_tensor("s_mg1", [NEXP, TC], BF16, kind="Internal").ap()
    s_bmc = nc.dram_tensor("s_bmc", [NST, 2 * TC], BF16, kind="Internal").ap()

    stack = ExitStack()
    SB = lambda name, shape, dt: stack.enter_context(nc.sbuf_tensor(name, shape, dt))
    PS = lambda name: stack.enter_context(nc.psum_tensor(name, [128, 512], F32))

    residR = [SB(f"residR{i}", [128, KD * TC], F32) for i in range(4)]
    scr2 = SB("scr2", [128, 2 * KD * TC], BF16)
    scr2b = SB("scr2b", [128, KD * TC], BF16)
    xln0 = scr2[:, 0:KD * TC]
    xln1 = scr2b[:, :]
    carry = SB("carry", [128, 16 * 4], BF16)
    xs = SB("xs", [128, KI * TC], F32)
    xsbf = SB("xsbf", [128, KI * TC], BF16)
    dtb_ = SB("dt", [128, KI * TC], BF16)
    ubf = SB("ubf", [128, KI * TC], BF16)
    sres = SB("sres", [128, KI * TC], BF16)
    oop = SB("oop", [128, KI * TC], BF16)
    egb = SB("egb", [128, 3 * TC], F32)
    wbuf = SB("wbuf", [128, NST * TC], BF16)
    hbuf = SB("hbuf", [128, NST * TC], BF16)
    ypr = SB("ypr", [128, NST * TC], BF16)
    yrow = SB("yrow", [128, TC], F32)
    state = SB("state", [128, BPC * 16 * NST], F32)
    brep = SB("brep", [128, NST * TC], BF16)
    crep = SB("crep", [128, NST * TC], BF16)
    spscr = SB("spscr", [128, 6 * TC], F32)
    yrow2 = SB("yrow2", [128, TC], F32)
    xe_sb = SB("xe_sb", [128, 2 * NEXP * TC], BF16)
    lncast = SB("lncast", [128, 2 * KD * TC], BF16)
    stats_t = SB("stats_t", [1, 1024], F32)
    xtmp = xs[:, 0:KD * TC]
    ipj_w = [SB(f"ipj_w{i}", [128, KD * 128], BF16) for i in range(2)]
    dtw_w = [SB(f"dtw_w{i}", [128, KD * 128], BF16) for i in range(3)]
    outw_w = [SB(f"outw_w{i}", [128, KD * 128], BF16) for i in range(3)]
    exp_w = [SB(f"exp_w{i}", [128, DIM], BF16) for i in range(3)]
    rw_bf = SB("rw_bf", [128, KD * NEXP], BF16)
    rb_bf = SB("rb_bf", [1, NEXP], BF16)
    eb_bf = SB("eb_bf", [NEXP, DIM], BF16)
    bw_bf = SB("bw_bf", [128, KI * NST], BF16)
    cw_bf = SB("cw_bf", [128, KI * NST], BF16)
    acol = SB("acol", [128, 16 * NST], F32)
    convw = SB("convw_s", [128, 16 * 4], F32)
    convb = SB("convb_s", [128, 16], F32)
    dtbias = SB("dtbias", [128, 16], F32)
    eye_sb = SB("eye_sb", [128, 128], F32)
    ones1f = SB("ones1f_s", [1, 128], F32)
    onescol = SB("onescol_s", [128, 1], BF16)
    onesrow = SB("onesrow_s", [1, TC], BF16)
    stats = stats_t[:, :]
    xbf = lncast[:, 0:KD * TC]
    xsq = lncast[:, KD * TC:2 * KD * TC]
    wbuf2 = xs[:, 0:KI * TC // 2].bitcast(BF16)
    hbuf2 = xs[:, KI * TC // 2:KI * TC].bitcast(BF16)
    wbufs = [wbuf, wbuf2]
    hbufs = [hbuf, hbuf2]
    yprs = [ypr, ypr]
    yrows = [yrow, yrow2]
    cvals = SB("cvals_s", [128, 4], F32)
    zsm = SB("zsm", [128, 8], F32)
    bmcbf = SB("bmcbf", [NST, 2 * TC], BF16)
    rsets = []
    for si in range(2):
        rsets.append({
            "lgsb": SB(f"lgsb{si}", [NEXP, TC], F32),
            "ltm": SB(f"ltm{si}", [128, 16], F32),
            "exrow": SB(f"exrow{si}", [128, 16], F32),
            "mrow": SB(f"mrow{si}", [128, 16], F32),
            "msrow": SB(f"msrow{si}", [128, 16], F32),
            "mgtm": SB(f"mgtm{si}", [128, 16], F32),
            "rmax": SB(f"rmax{si}", [128, 2], F32),
            "nrm": SB(f"nrm{si}", [128, 2], F32),
            "ssum": SB(f"ssum{si}", [128, 2], F32),
            "g05": SB(f"g05{si}", [128, 2], F32),
            "mgbf": SB(f"mgbf{si}", [NEXP, TC], BF16),
            "mgrep": scr2[:, KD * TC:2 * KD * TC],
            "s_mg": None,
        })
    prep_f = [xs, xs]
    prep_b = [ypr, wbuf]

    rsets[0]["s_mg"] = s_mg0
    rsets[1]["s_mg"] = s_mg1
    extb = xe_sb[:, 0:KI * TC]
    xe_view = lambda slot, e: xe_sb[:, (slot * NEXP + e) * TC:(slot * NEXP + e + 1) * TC]

    pb = [PS(f"pb{i}") for i in range(4)]
    pstat = PS("pstat")
    prep_p = PS("prep_p")
    pbm = PS("pbm")
    pmisc = PS("pmisc")
    ffn_banks = pb + [pstat, prep_p, pbm, pmisc]

    nc.const_aps.register(F32, EPS, cvals[:, 0:1])
    nc.const_aps.register(F32, -0.5, cvals[:, 1:2])
    nc.const_aps.register(F32, 1.0, cvals[:, 2:3])
    nc.const_aps.register(F32, -1.0, cvals[:, 3:4])

    sems = {}
    for s in ["qs", "qc", "qg", "pe", "ve", "se", "od", "ge"]:
        sems[s] = stack.enter_context(nc.semaphore(s))

    prog = {k: [] for k in ["sync", "vector", "scalar", "gpsimd", "tensor"]}
    C = {k: 0 for k in sems}
    LW = {}

    def emit(eng, fn, inc=None, n=1):
        if inc is not None:
            C[inc] += n
            sem = sems[inc]
            prog[eng].append(lambda e, fn=fn, sem=sem, n=n: fn(e).then_inc(sem, n))
            return C[inc]
        prog[eng].append(fn)
        return None

    def wait(eng, sem_name, val):
        if val is None or val <= 0:
            return
        key = (eng, sem_name)
        if LW.get(key, -1) >= val:
            return
        LW[key] = val
        sem = sems[sem_name]
        prog[eng].append(lambda e, sem=sem, val=val: e.wait_ge(sem, val))

    QENG = {"qs": "sync", "qc": "scalar", "qg": "gpsimd"}

    def dma(q, out, in_, slow=False):
        C[q] += 16
        sem = sems[q]
        prog[QENG[q]].append(lambda e, out=out, in_=in_, sem=sem, slow=slow: e.dma_start(
            out=out, in_=in_, allow_slow_non_contiguous=slow).then_inc(sem, 16))
        return C[q]

    def vdrain():
        prog["vector"].append(lambda e: e.drain())

    small_loads = []

    def ld_small(out_ap, in_ap, slow=False):
        small_loads.append(dma("qs", out_ap, in_ap, slow=slow))

    ld_small(eye_sb[:, :], d_eye)
    ld_small(ones1f[:, :], d_ones1f)
    ld_small(onescol[:, :], d_onescol)
    ld_small(onesrow[:, :], d_onesrow)
    ld_small(cvals[:, :], d_cvals)
    stage_f = prep_f[0]
    ld_small(stage_f[0:1, 0:NEXP], d_rb)
    ld_small(stage_f[0:NEXP, 1024:1024 + DIM], d_eb)
    ld_small(stage_f[:, 2048:2048 + KD * NEXP].rearrange("p (kt e) -> p kt e", kt=KD),
             d_rw.rearrange("(kt p) e -> p kt e", p=128))
    ld_small(stage_f[:, 2112:2112 + KI * NST].rearrange("p (kt n) -> p kt n", kt=KI),
             d_BwT.rearrange("(kt p) n -> p kt n", p=128))
    ld_small(stage_f[:, 2368:2368 + KI * NST].rearrange("p (kt n) -> p kt n", kt=KI),
             d_CwT.rearrange("(kt p) n -> p kt n", p=128))
    ld_small(stage_f[:, 2624:2624 + 16 * NST].rearrange("p (dt n) -> p dt n", dt=16),
             d_alog.rearrange("(dt p) n -> p dt n", p=128))
    ld_small(convw[:, :].rearrange("p (dt k) -> p dt k", dt=16),
             d_convw.rearrange("(dt p) k -> p dt k", p=128))
    ld_small(convb[:, :], d_convb.rearrange("(dt p) -> p dt", p=128), slow=True)
    ld_small(dtbias[:, :], d_dtb.rearrange("(dt p) -> p dt", p=128), slow=True)
    smalls_loaded = C["qs"]

    wait("vector", "qs", smalls_loaded)
    emit("vector", lambda e: e.tensor_copy(out=rb_bf[:, :], in_=stage_f[0:1, 0:NEXP]))
    emit("vector", lambda e: e.tensor_copy(out=eb_bf[:, :], in_=stage_f[0:NEXP, 1024:1024 + DIM]))
    emit("vector", lambda e: e.tensor_copy(out=rw_bf[:, :], in_=stage_f[:, 2048:2048 + KD * NEXP]))
    emit("vector", lambda e: e.tensor_copy(out=bw_bf[:, :], in_=stage_f[:, 2112:2112 + KI * NST]))
    emit("vector", lambda e: e.tensor_copy(out=cw_bf[:, :], in_=stage_f[:, 2368:2368 + KI * NST]))
    emit("vector", lambda e: e.memset(zsm[:, :], 0.0))
    emit("vector", lambda e: e.memset(state[:, :], 0.0))
    emit("vector", lambda e: e.memset(carry[:, :], 0.0))
    smalls_cast = emit("vector", lambda e: e.memset(yrow[:, :], 0.0), inc="ve")
    wait("scalar", "qs", smalls_loaded)
    acol_exp = emit("scalar", lambda e: e.activation(out=acol[:, :], in_=stage_f[:, 2624:2624 + 16 * NST], func=AF.Exp), inc="se")
    wait("vector", "se", acol_exp)
    acol_done = emit("vector", lambda e: e.tensor_scalar(out=acol[:, :], in0=acol[:, :], scalar1=-1.0, scalar2=None, op0=ALU.mult), inc="ve")

    prep_chunks = []
    for k in range(KD):
        src = d_ipjT[k * 128:(k + 1) * 128, :]
        writes = [((m * 128, (m + 1) * 128), s_ipj[m, :, k * 128:(k + 1) * 128]) for m in range(32)]
        prep_chunks.append((src, 4096, writes))
    for k in range(KI):
        src = d_dtwT[k * 128:(k + 1) * 128, :]
        writes = [((m * 128, (m + 1) * 128), s_dtw[m, :, k * 128:(k + 1) * 128]) for m in range(16)]
        prep_chunks.append((src, 2048, writes))
    for k in range(KI):
        src = d_outwT[k * 128:(k + 1) * 128, :]
        writes = [((0, 1024), s_outw[k, :, :])]
        prep_chunks.append((src, 1024, writes))
    for e in range(NEXP):
        for q in range(2):
            src = d_ew[e, 512 * q:512 * (q + 1), :].rearrange("(kt p) c -> p kt c", p=128)
            writes = [((i * 1024, (i + 1) * 1024), s_exp[e, 4 * q + i, :, :]) for i in range(4)]
            prep_chunks.append((src, 4096, writes))

    wait("sync", "ve", max(smalls_cast, acol_done))
    wait("sync", "se", acol_exp)
    slot_store_done = [0, 0]
    prev_cast = 0
    for i, (src, fd, writes) in enumerate(prep_chunks):
        sl = i % 2
        wait("sync", "ve", prev_cast)
        wait("sync", "qc", slot_store_done[sl])
        dst = prep_f[0][:, 0:fd]
        if len(src.shape) == 3:
            dst = dst.rearrange("p (a b) -> p a b", a=src.shape[1])
        ldv = dma("qs", dst, src)
        wait("vector", "qs", ldv)
        cv = emit("vector", lambda e, sl=sl, fd=fd: e.tensor_copy(out=prep_b[sl][:, 0:fd], in_=prep_f[0][:, 0:fd]), inc="ve")
        prev_cast = cv
        wait("scalar", "ve", cv)
        for (c0, c1), dst in writes:
            slot_store_done[sl] = dma("qc", dst, prep_b[sl][:, c0:c1])
    PREP_QC = C["qc"]
    PREP_VE = C["ve"]

    bank_ready_pe = [0] * 4
    bank_free = [("ve", 0)] * 4
    st_free = {"pstat": ("ve", 0), "prep": ("ve", 0), "pbm": ("ve", 0), "pmisc": ("ve", 0)}
    wslot_free = {"ipj": [("pe", 0)] * 2, "dtw": [("pe", 0)] * 3, "outw": [("pe", 0)] * 3,
                  "exp": [("pe", 0)] * 3}
    wslot_loaded = {"ipj": [0] * 2, "dtw": [0] * 3, "outw": [0] * 3, "exp": [0] * 3}
    wcount = {"ipj": 0, "dtw": 0, "outw": 0, "exp": 0}
    WQ = {"ipj": "qc", "dtw": "qg", "outw": "qs", "exp": "qs"}
    WBUFS = {"ipj": ipj_w, "dtw": dtw_w, "outw": outw_w, "exp": exp_w}

    first_stream_wait = {"qc": PREP_QC, "qs": PREP_QC}

    def stream_weight(kind, src_ap):
        i = wcount[kind]
        wcount[kind] += 1
        bufs = WBUFS[kind]
        sl = i % len(bufs)
        q = WQ[kind]
        qe = QENG[q]
        wait(qe, "qc", PREP_QC)
        eng_s, val = wslot_free[kind][sl]
        wait(qe, eng_s, val)
        v = dma(q, bufs[sl][:, :src_ap.shape[-1] if False else bufs[sl].shape[1]], src_ap)
        wslot_loaded[kind][sl] = v
        return bufs[sl], v, sl

    def mark_wslot_consumed(kind, sl, pe_val):
        wslot_free[kind][sl] = ("pe", pe_val)

    prev_oop15 = 0
    prev_mask_done = 0
    prev_store = 0


    NSLOT = 26
    ST = [dict() for _ in range(NSTAGE + 1)]

    def Xb(k):
        return residR[(2 * k) % 4]

    def Ab(k):
        return residR[(2 * k + 1) % 4]

    def run_slots(sched, lo, hi):
        for s in range(lo, min(hi, NSLOT)):
            for t in sched[s]:
                t()
            sched[s] = []

    def ln_units(k, src_fn, xlnb, sched, s0, tag, cast_guard_fn, out, cb=None, apply_guard_fn=None):
        xbf_, xsq_ = cb if cb is not None else (xbf, xsq)
        def u_cast():
            xbf, xsq = xbf_, xsq_
            src = src_fn()
            for sname, val in cast_guard_fn():
                wait("scalar", sname, val)
            emit("scalar", lambda e, src=src: e.copy(out=xbf, in_=src[:, :]))
            v_sq = emit("scalar", lambda e, src=src: e.activation(out=xsq, in_=src[:, :], func=AF.Square), inc="se")
            wait("tensor", "se", v_sq)
            e_s, v_s = st_free["pstat"]
            wait("tensor", e_s, v_s)
            for kk in range(KD):
                emit("tensor", lambda e, kk=kk, xbf=xbf: e.matmul(out=pstat[0:1, 0:TC], lhsT=onescol[:, :],
                                                                  rhs=xbf[:, kk * TC:(kk + 1) * TC],
                                                                  start=(kk == 0), stop=(kk == KD - 1)))
            for kk in range(KD):
                emit("tensor", lambda e, kk=kk, xsq=xsq: e.matmul(out=pstat[0:1, TC:2 * TC], lhsT=onescol[:, :],
                                                                  rhs=xsq[:, kk * TC:(kk + 1) * TC],
                                                                  start=(kk == 0), stop=(kk == KD - 1)),
                     inc=("pe" if kk == KD - 1 else None))
            out["v_stats"] = C["pe"]

        def u_math():
            wait("vector", "pe", out["v_stats"])
            emit("vector", lambda e: e.tensor_scalar(out=stats[0:1, 0:TC], in0=pstat[0:1, 0:TC],
                                                     scalar1=1.0 / DIM, scalar2=None, op0=ALU.mult))
            vb4 = emit("vector", lambda e: e.tensor_scalar(out=stats[0:1, 512:512 + TC], in0=pstat[0:1, TC:2 * TC],
                                                           scalar1=1.0 / DIM, scalar2=None, op0=ALU.mult), inc="ve")
            st_free["pstat"] = ("ve", vb4)
            vdrain()
            emit("vector", lambda e: e.tensor_tensor(out=stats[0:1, 768:768 + TC], in0=stats[0:1, 0:TC],
                                                     in1=stats[0:1, 0:TC], op=ALU.mult))
            vdrain()
            emit("vector", lambda e: e.tensor_tensor(out=stats[0:1, 512:512 + TC], in0=stats[0:1, 512:512 + TC],
                                                     in1=stats[0:1, 768:768 + TC], op=ALU.subtract))
            vdrain()
            v_var = emit("vector", lambda e: e.tensor_scalar(out=stats[0:1, 512:512 + TC], in0=stats[0:1, 512:512 + TC],
                                                             scalar1=0.0, scalar2=None, op0=ALU.max), inc="ve")
            wait("scalar", "ve", v_var)
            emit("scalar", lambda e: e.activation(out=stats[0:1, 256:256 + TC], in_=stats[0:1, 512:512 + TC],
                                                  func=AF.Ln, bias=EPS))
            v_rstd = emit("scalar", lambda e: e.activation(out=stats[0:1, 256:256 + TC], in_=stats[0:1, 256:256 + TC],
                                                           func=AF.Exp, scale=-0.5), inc="se")
            wait("tensor", "se", v_rstd)
            e_s, v_s = st_free["prep"]
            wait("tensor", e_s, v_s)
            emit("tensor", lambda e: e.matmul(out=prep_p[:, 0:TC], lhsT=ones1f[:, :], rhs=stats[0:1, 0:TC],
                                              start=True, stop=True))
            emit("tensor", lambda e: e.matmul(out=prep_p[:, TC:2 * TC], lhsT=ones1f[:, :],
                                              rhs=stats[0:1, 256:256 + TC], start=True, stop=True), inc="pe")
            out["v_reps"] = C["pe"]

        def u_apply():
            src = src_fn()
            wait("vector", "pe", out["v_reps"])
            if apply_guard_fn is not None:
                for sname, val in apply_guard_fn():
                    wait("vector", sname, val)
            for kk in range(KD):
                emit("vector", lambda e, kk=kk, src=src, xsq=xsq_: e.tensor_tensor(
                    out=xsq[:, kk * TC:(kk + 1) * TC], in0=src[:, kk * TC:(kk + 1) * TC],
                    in1=prep_p[:, 0:TC], op=ALU.subtract))
            vdrain()
            for kk in range(KD):
                emit("vector", lambda e, kk=kk, xsq=xsq_: e.tensor_tensor(
                    out=xlnb[:, kk * TC:(kk + 1) * TC], in0=xsq[:, kk * TC:(kk + 1) * TC],
                    in1=prep_p[:, TC:2 * TC], op=ALU.mult),
                     inc=("ve" if kk == KD - 1 else None))
            out["v_xln"] = C["ve"]
            st_free["prep"] = ("ve", out["v_xln"])

        sched[s0].append(u_cast)
        sched[s0 + 1].append(u_math)
        sched[s0 + 2].append(u_apply)

    def router_units(k, xlnb, rset, sched, s0, out):
        R = rsets[rset]

        def u_logits():
            e_s, v_s = st_free["pmisc"]
            wait("tensor", e_s, v_s)
            wait("tensor", "ve", out["v_xln"])
            for kk in range(KD):
                emit("tensor", lambda e, kk=kk: e.matmul(out=pmisc[0:NEXP, 0:TC],
                                                         lhsT=rw_bf[:, kk * NEXP:(kk + 1) * NEXP],
                                                         rhs=xlnb[:, kk * TC:(kk + 1) * TC],
                                                         start=(kk == 0), stop=False))
            v_lg = emit("tensor", lambda e: e.matmul(out=pmisc[0:NEXP, 0:TC], lhsT=rb_bf[:, :],
                                                     rhs=onesrow[:, :], start=False, stop=True), inc="pe")
            wait("vector", "pe", v_lg)
            v_b7 = emit("vector", lambda e: e.tensor_copy(out=R["lgsb"][:, :], in_=pmisc[0:NEXP, 0:TC]), inc="ve")
            e_s, v_s = st_free["pstat"]
            wait("tensor", e_s, v_s)
            wait("tensor", "ve", v_b7)
            emit("tensor", lambda e: e.transpose(out=pstat[:, 0:8], in_=R["lgsb"][0:NEXP, 0:128],
                                                 identity=eye_sb[0:8, 0:8]))
            v_ltm = emit("tensor", lambda e: e.transpose(out=pstat[:, 8:16], in_=R["lgsb"][0:NEXP, 128:256],
                                                         identity=eye_sb[0:8, 0:8]), inc="pe")
            out["v_ltm"] = v_ltm
            out["v_b7"] = v_b7

        def u_soft():
            R_ = R
            wait("vector", "pe", out["v_ltm"])
            v_evac = emit("vector", lambda e: e.tensor_copy(out=R_["ltm"][:, :], in_=pstat[:, 0:16]), inc="ve")
            st_free["pstat"] = ("ve", v_evac)
            vdrain()
            for c in range(2):
                sl = slice(c * 8, (c + 1) * 8)
                emit("vector", lambda e, sl=sl, c=c: e.reduce_max(R_["rmax"][:, c:c + 1], R_["ltm"][:, sl], axis=AX.X))
            vdrain()
            for c in range(2):
                emit("vector", lambda e, c=c: e.tensor_scalar(out=R_["nrm"][:, c:c + 1], in0=R_["rmax"][:, c:c + 1],
                                                              scalar1=-1.0, scalar2=None, op0=ALU.mult),
                     inc=("ve" if c == 1 else None))
            v_nrm = C["ve"]
            wait("scalar", "ve", v_nrm)
            for c in range(2):
                sl = slice(c * 8, (c + 1) * 8)
                emit("scalar", lambda e, sl=sl, c=c: e.activation(out=R_["exrow"][:, sl], in_=R_["ltm"][:, sl],
                                                                  func=AF.Exp, bias=R_["nrm"][:, c:c + 1]),
                     inc=("se" if c == 1 else None))
            out["v_ex"] = C["se"]

        def u_mask():
            R_ = R
            wait("vector", "se", out["v_ex"])
            for c in range(2):
                sl = slice(c * 8, (c + 1) * 8)
                emit("vector", lambda e, sl=sl, c=c: e.reduce_sum(R_["ssum"][:, c:c + 1], R_["exrow"][:, sl], axis=AX.X))
            vdrain()
            for c in range(2):
                emit("vector", lambda e, c=c: e.reciprocal(R_["g05"][:, c:c + 1], R_["ssum"][:, c:c + 1]))
            vdrain()
            for c in range(2):
                emit("vector", lambda e, c=c: e.tensor_scalar(out=R_["g05"][:, c:c + 1], in0=R_["g05"][:, c:c + 1],
                                                              scalar1=0.5, scalar2=None, op0=ALU.mult))
            for c in range(2):
                sl = slice(c * 8, (c + 1) * 8)
                emit("vector", lambda e, sl=sl, c=c: e.tensor_scalar(out=R_["mrow"][:, sl], in0=R_["ltm"][:, sl],
                                                                     scalar1=R_["rmax"][:, c:c + 1], scalar2=None,
                                                                     op0=ALU.is_equal))
            vdrain()
            for c in range(2):
                sl = slice(c * 8, (c + 1) * 8)
                emit("vector", lambda e, sl=sl: e.tensor_tensor_scan(out=R_["msrow"][:, sl], data0=R_["mrow"][:, sl],
                                                                     data1=zsm[:, :], initial=0.0,
                                                                     op0=ALU.add, op1=ALU.add))
            vdrain()
            for c in range(2):
                sl = slice(c * 8, (c + 1) * 8)
                emit("vector", lambda e, sl=sl: e.tensor_scalar(out=R_["msrow"][:, sl], in0=R_["msrow"][:, sl],
                                                                scalar1=1.0, scalar2=None, op0=ALU.is_equal))
            vdrain()
            for c in range(2):
                sl = slice(c * 8, (c + 1) * 8)
                emit("vector", lambda e, sl=sl: e.tensor_tensor(out=R_["mrow"][:, sl], in0=R_["mrow"][:, sl],
                                                                in1=R_["msrow"][:, sl], op=ALU.mult))
            vdrain()
            for c in range(2):
                sl = slice(c * 8, (c + 1) * 8)
                emit("vector", lambda e, sl=sl, c=c: e.tensor_scalar(out=R_["mgtm"][:, sl], in0=R_["mrow"][:, sl],
                                                                     scalar1=R_["g05"][:, c:c + 1], scalar2=None,
                                                                     op0=ALU.mult), inc=("ve" if c == 1 else None))
            v_mgtm = C["ve"]
            e_s, v_s = st_free["pmisc"]
            wait("tensor", e_s, v_s)
            wait("tensor", "ve", v_mgtm)
            for c in range(2):
                emit("tensor", lambda e, c=c: e.transpose(out=pmisc[0:8, c * 128:(c + 1) * 128],
                                                          in_=R_["mgtm"][:, c * 8:(c + 1) * 8],
                                                          identity=eye_sb[:, :]), inc=("pe" if c == 1 else None))
            v_mgfm = C["pe"]
            wait("vector", "pe", v_mgfm)
            v_mgbf = emit("vector", lambda e: e.tensor_copy(out=R_["mgbf"][:, :], in_=pmisc[0:8, 0:TC]), inc="ve")
            st_free["pmisc"] = ("ve", v_mgbf)
            out["v_mgbf"] = v_mgbf
            wait("gpsimd", "ve", v_mgbf)
            v_w = dma("qg", R_["s_mg"], R_["mgbf"][:, :])
            wait("gpsimd", "qg", v_w)
            wait("gpsimd", "ve", mgrep_free[0])
            rep_src = R_["s_mg"].rearrange("e t -> (e t)").unsqueeze(0).broadcast_to((128, NEXP * TC))
            out["v_mgrep"] = dma("qg", R_["mgrep"][:, :], rep_src)

        sched[s0].append(u_logits)
        sched[s0 + 1].append(u_soft)
        sched[s0 + 2].append(u_mask)

    xe_last_pe = [0, 0]
    mgrep_free = [0]

    def ffn_units(k, xlnb, rset, src_fn, dst_fn, sched, s_mask0, s_evac, out):
        R = rsets[rset]
        xe_cons = xe_last_pe

        def mk_group(kk):
            def u_group():
                wait("vector", "qg", out["v_mgrep"])
                wait("vector", "pe", xe_cons[kk % 2])
                wait("vector", "ve", out["v_xln"])
                for e_i in range(NEXP):
                    emit("vector", lambda e, kk=kk, e_i=e_i: e.tensor_tensor(
                        out=xe_view(kk % 2, e_i), in0=xlnb[:, kk * TC:(kk + 1) * TC],
                        in1=R["mgrep"][:, e_i * TC:(e_i + 1) * TC], op=ALU.mult),
                        inc=("ve" if e_i == NEXP - 1 else None))
                xe_done = C["ve"]
                wait("tensor", "ve", xe_done)
                if kk == 0:
                    for b in range(4):
                        e_s, v_s = bank_free[b]
                        wait("tensor", e_s, v_s)
                    for key in ("pstat", "prep", "pbm", "pmisc"):
                        e_s, v_s = st_free[key]
                        wait("tensor", e_s, v_s)
                    wait("tensor", "ve", out["v_mgbf"])
                    for m in range(8):
                        emit("tensor", lambda e, m=m: e.matmul(out=ffn_banks[m][:, 0:TC],
                                                               lhsT=eb_bf[:, m * 128:(m + 1) * 128],
                                                               rhs=R["mgbf"][:, :], start=True, stop=False))
                for e_i in range(NEXP):
                    buf, ldv, sl = stream_weight("exp", s_exp[e_i, kk, :, :])
                    wait("tensor", WQ["exp"], ldv)
                    for m in range(8):
                        lastmm = (kk == KD - 1) and (e_i == NEXP - 1)
                        emit("tensor", lambda e, m=m, buf=buf, lastmm=lastmm, kk=kk, e_i=e_i: e.matmul(
                            out=ffn_banks[m][:, 0:TC], lhsT=buf[:, m * 128:(m + 1) * 128],
                            rhs=xe_view(kk % 2, e_i), start=False, stop=lastmm,
                            skip_group_check=True),
                            inc=("pe" if m == 7 else None))
                    mark_wslot_consumed("exp", sl, C["pe"])
                xe_cons[kk % 2] = C["pe"]
                if kk == KD - 1:
                    out["v_ffn_pe"] = C["pe"]
                    mgrep_free[0] = xe_done
            return u_group

        for kk in range(KD):
            sched[s_mask0 + kk].append(mk_group(kk))

        def u_evac():
            src = src_fn()
            dst = dst_fn()
            wait("vector", "pe", out["v_ffn_pe"])
            if rset == 0 and k >= 2:
                wait("vector", "od", ST[k - 2]["v_store"])
            for m in range(8):
                emit("vector", lambda e, m=m, src=src, dst=dst: e.scalar_tensor_tensor(
                    out=dst[:, m * TC:(m + 1) * TC], in0=ffn_banks[m][:, 0:TC], scalar=1.0,
                    in1=src[:, m * TC:(m + 1) * TC], op0=ALU.mult, op1=ALU.add),
                    inc=("ve" if m == 7 else None))
            out["v_evac"] = C["ve"]
            for b in range(4):
                bank_free[b] = ("ve", out["v_evac"])
            for key in ("pstat", "prep", "pbm", "pmisc"):
                st_free[key] = ("ve", out["v_evac"])

        sched[s_evac].append(u_evac)

    def plan_front(k, sched):
        st = ST[k]
        st["ln1"] = {}
        st["r1"] = {}
        st["f1"] = {}
        st["ipx"] = [0] * 16
        st["sres"] = [0] * 16
        s = k // (L // TC)
        c = k % (L // TC)
        st["s"], st["c"] = s, c
        x_src = d_x[s, :, c * TC:(c + 1) * TC].rearrange("(dp p) t -> p dp t", p=128)

        def u_xload():
            if k >= 2:
                wait("sync", "ve", ST[k - 2]["f2"]["v_evac"])
                wait("sync", "od", ST[k - 2]["v_store"])
            st["v_x"] = dma("qs", Xb(k)[:, :].rearrange("p (dp t) -> p dp t", dp=KD), x_src)
        sched[2].append(u_xload)

        def cast_guard1():
            return [("qs", st["v_x"])]

        def apply_guard1():
            if k >= 1:
                return [("pe", ST[k - 1]["ipj_pe"])]
            return []
        ln_units(k, lambda: Xb(k), xln0, sched, 3, "ln1", cast_guard1, st["ln1"],
                 apply_guard_fn=apply_guard1)
        st["r1"]["v_xln"] = None

        def link_r1():
            st["r1"]["v_xln"] = st["ln1"]["v_xln"]
        sched[5].append(link_r1)
        router_units(k, xln0, 0, sched, 5, st["r1"])
        st["f1"]["v_mgrep"] = None

        def link_f1():
            st["f1"]["v_mgrep"] = st["r1"]["v_mgrep"]
            st["f1"]["v_mgbf"] = st["r1"]["v_mgbf"]
            st["f1"]["v_xln"] = st["ln1"]["v_xln"]
        sched[7].append(link_f1)
        ffn_units(k, xln0, 0, lambda: Xb(k), lambda: Ab(k), sched, 8, 16, st["f1"])

        st["ln2"] = {}

        def cast_guard2():
            return [("ve", st["f1"]["v_evac"])]

        if DEBUG_DUMP:
            def u_dbg1():
                dbg1_dst = d_dbg1[s, :, c * TC:(c + 1) * TC].rearrange("(dp p) t -> p dp t", p=128)
                wait("sync", "ve", st["f1"]["v_evac"])
                C["od"] += 16
                prog["sync"].append(lambda e, dbg1_dst=dbg1_dst: e.dma_start(
                    out=dbg1_dst, in_=Ab(k)[:, :].rearrange("p (dp t) -> p dp t", dp=KD)).then_inc(sems["od"], 16))
                C["od"] += 16
                prog["sync"].append(lambda e, k=k: e.dma_start(
                    out=d_dbg3[k], in_=rsets[0]["mgbf"][:, :]).then_inc(sems["od"], 16))
                C["od"] += 16
                prog["sync"].append(lambda e, k=k: e.dma_start(
                    out=d_dbg4[k], in_=rsets[0]["mgrep"][0:1, :]).then_inc(sems["od"], 16))
            sched[17].append(u_dbg1)
        ln_units(k, lambda: Ab(k), xln0, sched, 16, "ln2", cast_guard2, st["ln2"])

        def u_ipj():
            for m in range(32):
                buf, ldv, sl = stream_weight("ipj", s_ipj[m, :, :])
                b = m % 4
                e_s, v_s = bank_free[b]
                wait("tensor", e_s, v_s)
                wait("tensor", WQ["ipj"], ldv)
                wait("tensor", "ve", st["ln2"]["v_xln"])
                for kk in range(KD):
                    emit("tensor", lambda e, b=b, kk=kk, buf=buf: e.matmul(
                        out=pb[b][:, 0:TC], lhsT=buf[:, kk * 128:(kk + 1) * 128],
                        rhs=xln0[:, kk * TC:(kk + 1) * TC], start=(kk == 0), stop=(kk == KD - 1)),
                        inc=("pe" if kk == KD - 1 else None))
                mark_wslot_consumed("ipj", sl, C["pe"])
                v_pe = C["pe"]
                if m < 16:
                    wait("scalar", "pe", v_pe)
                    v_e = emit("scalar", lambda e, m=m, b=b: e.copy(
                        out=extb[:, m * TC:(m + 1) * TC], in_=pb[b][:, 0:TC]), inc="se")
                    st["ipx"][m] = v_e
                    bank_free[b] = ("se", v_e)
                else:
                    mm = m - 16
                    wait("scalar", "pe", v_pe)
                    if k >= 1:
                        wait("scalar", "ve", ST[k - 1]["oop_done"][mm])
                    v_e = emit("scalar", lambda e, mm=mm, b=b: e.activation(
                        out=sres[:, mm * TC:(mm + 1) * TC], in_=pb[b][:, 0:TC], func=AF.Silu), inc="se")
                    st["sres"][mm] = v_e
                    bank_free[b] = ("se", v_e)
            st["ipj_pe"] = C["pe"]
        sched[18].append(u_ipj)

        def mk_conv(level):
            def u_conv():
                nc_ = 3 - level
                if level == 0:
                    wait("vector", "se", max(st["ipx"]))
                    vdrain()
                    if c == 0:
                        emit("vector", lambda e: e.memset(carry[:, :], 0.0))
                        vdrain()
                for j in range(16):
                    if nc_ > 0:
                        if level == 0:
                            emit("vector", lambda e, j=j, nc_=nc_: e.tensor_scalar(
                                out=xs[:, j * TC:j * TC + nc_], in0=carry[:, j * 4 + level:j * 4 + 3],
                                scalar1=convw[:, j * 4 + level:j * 4 + level + 1],
                                scalar2=convb[:, j:j + 1], op0=ALU.mult, op1=ALU.add))
                        else:
                            emit("vector", lambda e, j=j, nc_=nc_, level=level: e.scalar_tensor_tensor(
                                out=xs[:, j * TC:j * TC + nc_], in0=carry[:, j * 4 + level:j * 4 + 3],
                                scalar=convw[:, j * 4 + level:j * 4 + level + 1],
                                in1=xs[:, j * TC:j * TC + nc_], op0=ALU.mult, op1=ALU.add))
                    if level == 0:
                        emit("vector", lambda e, j=j, nc_=nc_: e.tensor_scalar(
                            out=xs[:, j * TC + nc_:(j + 1) * TC], in0=extb[:, j * TC:(j + 1) * TC - nc_],
                            scalar1=convw[:, j * 4:j * 4 + 1], scalar2=convb[:, j:j + 1],
                            op0=ALU.mult, op1=ALU.add))
                    else:
                        emit("vector", lambda e, j=j, nc_=nc_, level=level: e.scalar_tensor_tensor(
                            out=xs[:, j * TC + nc_:(j + 1) * TC], in0=extb[:, j * TC:(j + 1) * TC - nc_],
                            scalar=convw[:, j * 4 + level:j * 4 + level + 1],
                            in1=xs[:, j * TC + nc_:(j + 1) * TC], op0=ALU.mult, op1=ALU.add))
                vdrain()
                if level == 3:
                    for j in range(16):
                        emit("vector", lambda e, j=j: e.tensor_copy(out=carry[:, j * 4:j * 4 + 3],
                                                                    in_=extb[:, (j + 1) * TC - 3:(j + 1) * TC]),
                             inc=("ve" if j == 15 else None))
                    st["v_conv"] = C["ve"]
            return u_conv
        for lv in range(4):
            sched[19 + lv].append(mk_conv(lv))

        def u_silu():
            wait("scalar", "ve", st["v_conv"])
            for j in range(16):
                emit("scalar", lambda e, j=j: e.activation(out=xsbf[:, j * TC:(j + 1) * TC],
                                                           in_=xs[:, j * TC:(j + 1) * TC], func=AF.Silu),
                     inc=("se" if j == 15 else None))
            st["v_silu"] = C["se"]
        sched[23].append(u_silu)

    def plan_mid(k, sched):
        st = ST[k]
        st["dt_done"] = [0] * 16
        st["u_done"] = [0] * 16

        def mk_dt(m, slot):
            def u_dt():
                bufA, ldvA, slA = stream_weight("dtw", s_dtw[m, :, 0:KD * 128])
                bufB, ldvB, slB = stream_weight("dtw", s_dtw[m, :, KD * 128:KI * 128])
                b = m % 4
                e_s, v_s = bank_free[b]
                wait("tensor", e_s, v_s)
                wait("tensor", WQ["dtw"], max(ldvA, ldvB))
                wait("tensor", "se", st["v_silu"])
                for kk in range(KI):
                    buf = bufA if kk < KD else bufB
                    emit("tensor", lambda e, b=b, kk=kk, buf=buf: e.matmul(
                        out=pb[b][:, 0:TC], lhsT=buf[:, (kk % KD) * 128:(kk % KD + 1) * 128],
                        rhs=xsbf[:, kk * TC:(kk + 1) * TC], start=(kk == 0), stop=(kk == KI - 1)),
                        inc=("pe" if kk == KI - 1 else None))
                mark_wslot_consumed("dtw", slA, C["pe"])
                mark_wslot_consumed("dtw", slB, C["pe"])
                v_pe = C["pe"]
                sp = m % 3
                spa = spscr[:, sp * 2 * TC: sp * 2 * TC + TC]
                spb = spscr[:, sp * 2 * TC + TC: (sp + 1) * 2 * TC]
                wait("scalar", "pe", v_pe)
                if m >= 3:
                    wait("scalar", "ve", st["dt_done"][m - 3])
                emit("scalar", lambda e, b=b, spa=spa: e.activation(
                    out=spa, in_=pb[b][:, 0:TC], func=AF.Abs, bias=dtbias[:, m:m + 1]))
                emit("scalar", lambda e, spa=spa: e.activation(out=spa, in_=spa, func=AF.Exp, scale=-1.0))
                emit("scalar", lambda e, spa=spa: e.activation(out=spa, in_=spa, func=AF.Ln, bias=1.0))
                v_sp = emit("scalar", lambda e, b=b, spb=spb: e.activation(
                    out=spb, in_=pb[b][:, 0:TC], func=AF.Relu, bias=dtbias[:, m:m + 1]), inc="se")
                bank_free[b] = ("se", v_sp)
                wait("vector", "se", v_sp)
                if k >= 1:
                    wait("vector", "se", ST[k - 1]["eg_val"][m][NST - 1])
                v_dt = emit("vector", lambda e, m=m, spa=spa, spb=spb: e.tensor_tensor(
                    out=dtb_[:, m * TC:(m + 1) * TC], in0=spa, in1=spb, op=ALU.add), inc="ve")
                st["dt_done"][m] = v_dt
            return u_dt

        for m in range(16):
            if m < 8:
                sched[25].append(mk_dt(m, 25))
            else:
                sched2 = st.setdefault("own_sched_hooks", {})
                sched2[m - 8] = mk_dt(m, m - 8)

        def u_bc():
            e_s, v_s = st_free["pbm"]
            wait("tensor", e_s, v_s)
            wait("tensor", "se", st["v_silu"])
            for kk in range(KI):
                emit("tensor", lambda e, kk=kk: e.matmul(out=pbm[0:NST, 0:TC],
                                                         lhsT=bw_bf[:, kk * NST:(kk + 1) * NST],
                                                         rhs=xsbf[:, kk * TC:(kk + 1) * TC],
                                                         start=(kk == 0), stop=(kk == KI - 1)))
            for kk in range(KI):
                emit("tensor", lambda e, kk=kk: e.matmul(out=pbm[0:NST, TC:2 * TC],
                                                         lhsT=cw_bf[:, kk * NST:(kk + 1) * NST],
                                                         rhs=xsbf[:, kk * TC:(kk + 1) * TC],
                                                         start=(kk == 0), stop=(kk == KI - 1)),
                     inc=("pe" if kk == KI - 1 else None))
            st["bc_pe"] = C["pe"]
            wait("vector", "pe", st["bc_pe"])
            v_bmc = emit("vector", lambda e: e.tensor_copy(out=bmcbf[:, :], in_=pbm[0:NST, 0:2 * TC]), inc="ve")
            st_free["pbm"] = ("ve", v_bmc)
            wait("gpsimd", "ve", v_bmc)
            v_wb = dma("qg", s_bmc, bmcbf[:, :])
            wait("gpsimd", "qg", v_wb)
            if k >= 1:
                wait("gpsimd", "ve", ST[k - 1]["wmul_last"])
                wait("gpsimd", "ve", ST[k - 1]["ymul_last"])
            st["v_brep"] = dma("qg", brep[:, :].rearrange("p (n t) -> p n t", n=NST),
                               s_bmc[:, 0:TC].unsqueeze(0).broadcast_to((128, NST, TC)))
            st["v_crep"] = dma("qg", crep[:, :].rearrange("p (n t) -> p n t", n=NST),
                               s_bmc[:, TC:2 * TC].unsqueeze(0).broadcast_to((128, NST, TC)))
        sched[25].append(u_bc)

    def emit_back(k, sched, sched_next):
        st = ST[k]
        s = st["s"]
        st["eg_val"] = [[0] * NST for _ in range(16)]
        scan_val = [[0] * NST for _ in range(16)]
        st["oop_done"] = [0] * 16
        wmul_val = [0] * 16
        ymul_val = [0] * 16
        red_val = [0] * 16
        soffs = [s * 256 + jj * NST for jj in range(16)]
        hooks2 = st.get("own_sched_hooks", {})

        def em_u(j):
            wait("vector", "ve", st["dt_done"][j])
            st["u_done"][j] = emit("vector", lambda e, j=j: e.tensor_tensor(
                out=ubf[:, j * TC:(j + 1) * TC], in0=dtb_[:, j * TC:(j + 1) * TC],
                in1=xsbf[:, j * TC:(j + 1) * TC], op=ALU.mult), inc="ve")

        def em_wmul(j):
            wait("vector", "ve", st["u_done"][j])
            wait("vector", "se", st["v_silu"])
            wait("vector", "qg", max(st["v_brep"], st["v_crep"]))
            u_b = ubf[:, j * TC:(j + 1) * TC].unsqueeze(1).broadcast_to((128, NST, TC))
            wb = wbufs[j % 2]
            wmul_val[j] = emit("vector", lambda e, u_b=u_b, wb=wb: e.tensor_tensor(
                out=wb[:, :].rearrange("p (n t) -> p n t", n=NST), in0=u_b,
                in1=brep[:, :].rearrange("p (n t) -> p n t", n=NST), op=ALU.mult), inc="ve")

        def em_scans(j):
            wait("vector", "ve", wmul_val[j])
            wait("vector", "se", st["v_silu"])
            wait("scalar", "ve", st["dt_done"][j])
            wb = wbufs[j % 2]
            hb = hbufs[j % 2]
            for n in range(NST):
                if n >= 3:
                    wait("scalar", "ve", scan_val[j][n - 3])
                elif j > 0:
                    wait("scalar", "ve", scan_val[j - 1][NST - 3 + n])
                sl4 = (n % 3) * TC
                st["eg_val"][j][n] = emit("scalar", lambda e, j=j, n=n, sl4=sl4: e.activation(
                    out=egb[:, sl4:sl4 + TC], in_=dtb_[:, j * TC:(j + 1) * TC],
                    func=AF.Exp, scale=acol[:, j * NST + n:j * NST + n + 1]), inc="se")
                wait("vector", "se", st["eg_val"][j][n])
                scan_val[j][n] = emit("vector", lambda e, n=n, j=j, wb=wb, hb=hb, sl4=sl4: e.tensor_tensor_scan(
                    out=hb[:, n * TC:(n + 1) * TC], data0=egb[:, sl4:sl4 + TC],
                    data1=wb[:, n * TC:(n + 1) * TC], initial=state[:, soffs[j] + n:soffs[j] + n + 1],
                    op0=ALU.mult, op1=ALU.add), inc="ve")

        def em_state_ymul(j):
            wait("vector", "ve", scan_val[j][NST - 1])
            hb = hbufs[j % 2]
            hview = hb[:, :].rearrange("p (n t) -> p n t", n=NST)
            emit("vector", lambda e, j=j, hview=hview: e.tensor_copy(
                out=state[:, soffs[j]:soffs[j] + NST], in_=hview[:, :, TC - 1:TC]))
            if j >= 1:
                wait("vector", "ge", red_val[j - 1])
            elif k >= 1:
                wait("vector", "ge", ST[k - 1]["red15_ge"])
            yp = yprs[j % 2]
            ymul_val[j] = emit("vector", lambda e, hview=hview, yp=yp: e.tensor_tensor(
                out=yp[:, 0:NST * TC].rearrange("p (n t) -> p n t", n=NST), in0=hview,
                in1=crep[:, :].rearrange("p (n t) -> p n t", n=NST), op=ALU.mult), inc="ve")

        def em_reduce(j):
            wait("gpsimd", "ve", ymul_val[j])
            if j >= 2:
                wait("gpsimd", "ve", st["oop_done"][j - 2])
            yp = yprs[j % 2]
            for width in (8, 4, 2):
                emit("gpsimd", lambda e, yp=yp, width=width: e.tensor_tensor(
                    out=yp[:, 0:width * TC], in0=yp[:, 0:width * TC],
                    in1=yp[:, width * TC:2 * width * TC], op=ALU.add))
            red_val[j] = emit("gpsimd", lambda e, yp=yp: e.tensor_tensor(
                out=yrows[j % 2][:, :], in0=yp[:, 0:TC], in1=yp[:, TC:2 * TC],
                op=ALU.add), inc="ge")

        def em_outop(j):
            wait("vector", "ge", red_val[j])
            wait("vector", "se", st["sres"][j])
            if j == 0 and k >= 1:
                wait("vector", "pe", ST[k - 1]["ln3_pe_guard"])
            st["oop_done"][j] = emit("vector", lambda e, j=j: e.tensor_tensor(
                out=oop[:, j * TC:(j + 1) * TC], in0=yrows[j % 2][:, :],
                in1=sres[:, j * TC:(j + 1) * TC], op=ALU.mult), inc="ve")

        vdrain()
        run_slots(sched, 24, NSLOT)
        em_u(0)
        em_u(1)
        em_u(2)
        em_wmul(0)
        em_wmul(1)
        for j in range(19):
            if j < 16:
                em_scans(j)
            if 1 <= j <= 16:
                em_state_ymul(j - 1)
            if 2 <= j <= 17:
                em_reduce(j - 2)
            if 3 <= j <= 18:
                em_outop(j - 3)
            if j + 3 <= 15:
                em_u(j + 3)
            if j + 2 <= 15:
                em_wmul(j + 2)
            if j in hooks2:
                hooks2[j]()
            run_slots(sched, j, j + 1)
        st["wmul_last"] = wmul_val[15]
        st["ymul_last"] = ymul_val[15]
        st["red15_ge"] = red_val[15]
        run_slots(sched, 19, 24)

        for b in range(4):
            e_s, v_s = bank_free[b]
            wait("tensor", e_s, v_s)
        for key in ("pstat", "prep", "pbm", "pmisc"):
            e_s, v_s = st_free[key]
            wait("tensor", e_s, v_s)
        for kk in range(KI):
            buf, ldv, sl = stream_weight("outw", s_outw[kk, :, :])
            wait("tensor", WQ["outw"], ldv)
            wait("tensor", "ve", st["oop_done"][kk])
            for m in range(8):
                emit("tensor", lambda e, kk=kk, m=m, buf=buf: e.matmul(
                    out=ffn_banks[m][:, 0:TC], lhsT=buf[:, m * 128:(m + 1) * 128],
                    rhs=oop[:, kk * TC:(kk + 1) * TC], start=(kk == 0), stop=(kk == KI - 1)),
                    inc=("pe" if m == 7 else None))
            mark_wslot_consumed("outw", sl, C["pe"])
        v_outmm = C["pe"]
        wait("vector", "pe", v_outmm)
        wait("vector", "ge", red_val[15])
        for m in range(8):
            emit("vector", lambda e, m=m: e.scalar_tensor_tensor(
                out=Xb(k)[:, m * TC:(m + 1) * TC], in0=ffn_banks[m][:, 0:TC], scalar=1.0,
                in1=Ab(k)[:, m * TC:(m + 1) * TC], op0=ALU.mult, op1=ALU.add), inc=("ve" if m == 7 else None))
        v_battn = C["ve"]
        for b in range(4):
            bank_free[b] = ("ve", v_battn)
        for key in ("pstat", "prep", "pbm", "pmisc"):
            st_free[key] = ("ve", v_battn)

        if DEBUG_DUMP:
            dbg2_dst = d_dbg2[s, :, st["c"] * TC:(st["c"] + 1) * TC].rearrange("(dp p) t -> p dp t", p=128)
            C["od"] += 16
            prog["sync"].append(lambda e, dbg2_dst=dbg2_dst, vb=v_battn: (
                e.wait_ge(sems["ve"], vb),
                e.dma_start(out=dbg2_dst, in_=Xb(k)[:, :].rearrange("p (dp t) -> p dp t", dp=KD)).then_inc(sems["od"], 16)))
        st["ln3"] = {}
        st["r2"] = {}
        st["f2"] = {}
        direct = [[] for _ in range(NSLOT)]

        def guard3():
            return [("ve", v_battn)]
        ln_units(k, lambda: Xb(k), xln1, direct, 0, "ln3", guard3, st["ln3"],
                 cb=(oop[:, 0:KD * TC], oop[:, KD * TC:2 * KD * TC]))
        run_slots(direct, 0, 3)
        st["ln3_pe_guard"] = C["pe"]

        def link_r2():
            st["r2"]["v_xln"] = st["ln3"]["v_xln"]
        direct[0].append(link_r2)
        router_units(k, xln1, 1, direct, 0, st["r2"])
        run_slots(direct, 0, 3)

        def link_f2():
            st["f2"]["v_mgrep"] = st["r2"]["v_mgrep"]
            st["f2"]["v_mgbf"] = st["r2"]["v_mgbf"]
            st["f2"]["v_xln"] = st["ln3"]["v_xln"]
        link_f2()
        ffn_units(k, xln1, 1, lambda: Xb(k), lambda: Ab(k), direct, 0, 8, st["f2"])
        run_slots(direct, 0, 8)

        def u_f2_evac_store():
            for t in direct[8]:
                t()
            out_dst = d_out[st["s"], :, st["c"] * TC:(st["c"] + 1) * TC].rearrange("(dp p) t -> p dp t", p=128)
            wait("sync", "ve", st["f2"]["v_evac"])
            C["od"] += 16
            prog["sync"].append(lambda e, out_dst=out_dst: e.dma_start(
                out=out_dst, in_=Ab(k)[:, :].rearrange("p (dp t) -> p dp t", dp=KD)).then_inc(sems["od"], 16))
            st["v_store"] = C["od"]
        sched_next[24].append(u_f2_evac_store)

    sched_cur = [[] for _ in range(NSLOT)]
    sched_next = [[] for _ in range(NSLOT)]
    for k in range(NSTAGE):
        plan_front(k, sched_cur)
        if k >= 1:
            emit_back(k - 1, sched_cur, sched_next)
        else:
            run_slots(sched_cur, 0, NSLOT)
        plan_mid(k, sched_cur if k == 0 else sched_next)
        if k == 0:
            run_slots(sched_cur, 0, NSLOT)
            hooks2 = ST[0].get("own_sched_hooks", {})
            for j in sorted(hooks2):
                hooks2[j]()
            ST[0]["own_sched_hooks"] = {}
        else:
            sched_cur, sched_next = sched_next, [[] for _ in range(NSLOT)]
    emit_back(NSTAGE - 1, sched_cur, sched_next)
    run_slots(sched_next, 0, NSLOT)
    if DEBUG_DUMP:
        wait("vector", "od", C["od"])
        fin = emit("vector", lambda e: e.tensor_copy(out=xs[:, 0:1], in_=xs[:, 0:1]), inc="ve")
        wait("sync", "ve", fin)
        for idx_, src_ in [(0, sres), (1, crep), (2, brep), (3, oop)]:
            C["od"] += 16
            prog["sync"].append(lambda e, idx_=idx_, src_=src_: e.dma_start(
                out=d_dbg5[idx_], in_=src_[:, :]).then_inc(sems["od"], 16))

    wait("sync", "od", C["od"])

    blockctx = nc.Block()
    block = blockctx.__enter__()

    @block.sync
    def _(eng):
        for f in prog["sync"]:
            f(eng)

    @block.vector
    def _(eng):
        for f in prog["vector"]:
            f(eng)

    @block.scalar
    def _(eng):
        for f in prog["scalar"]:
            f(eng)

    @block.gpsimd
    def _(eng):
        for f in prog["gpsimd"]:
            f(eng)

    @block.tensor
    def _(eng):
        for f in prog["tensor"]:
            f(eng)

    blockctx.__exit__(None, None, None)
    stack.close()
    return nc


def prepare(L0_attn_x, L0_attn_in_proj_w, L0_attn_conv_w, L0_attn_conv_b,
            L0_attn_dt_w, L0_attn_dt_b, L0_attn_A_log, L0_attn_B_w, L0_attn_C_w,
            L0_attn_out_w, L0_ffn_router_w, L0_ffn_router_b, L0_ffn_expert_w,
            L0_ffn_expert_b):
    x = np.asarray(L0_attn_x, dtype=np.float32)
    nc = build_nc()
    common = {
        "ipjT": np.ascontiguousarray(np.asarray(L0_attn_in_proj_w, np.float32).T),
        "convw": np.ascontiguousarray(np.asarray(L0_attn_conv_w, np.float32)),
        "convb": np.ascontiguousarray(np.asarray(L0_attn_conv_b, np.float32)),
        "dtwT": np.ascontiguousarray(np.asarray(L0_attn_dt_w, np.float32).T),
        "dtb": np.ascontiguousarray(np.asarray(L0_attn_dt_b, np.float32)),
        "alog": np.ascontiguousarray(np.asarray(L0_attn_A_log, np.float32)),
        "BwT": np.ascontiguousarray(np.asarray(L0_attn_B_w, np.float32).T),
        "CwT": np.ascontiguousarray(np.asarray(L0_attn_C_w, np.float32).T),
        "outwT": np.ascontiguousarray(np.asarray(L0_attn_out_w, np.float32).T),
        "rw": np.ascontiguousarray(np.asarray(L0_ffn_router_w, np.float32)),
        "rb": np.ascontiguousarray(np.asarray(L0_ffn_router_b, np.float32).reshape(1, NEXP)),
        "ew": np.ascontiguousarray(np.asarray(L0_ffn_expert_w, np.float32)),
        "eb": np.ascontiguousarray(np.asarray(L0_ffn_expert_b, np.float32)),
        "eye": np.eye(128, dtype=np.float32),
        "ones1f": np.ones((1, 128), np.float32),
        "onescol": np.ones((128, 1), ml_dtypes.bfloat16),
        "onesrow": np.ones((1, TC), ml_dtypes.bfloat16),
        "cvals": np.tile(np.array([[EPS, -0.5, 1.0, -1.0]], np.float32), (128, 1)),
    }
    in_maps = []
    for i in range(NCORES):
        xi = np.ascontiguousarray(x[BPC * i:BPC * (i + 1)].transpose(0, 2, 1))
        m = dict(common)
        m["x"] = xi
        in_maps.append(m)
    return nc, in_maps


def kernel(**inputs):
    nc, in_maps = prepare(**inputs)
    res = run_bass_kernel_spmd(nc, in_maps, core_ids=list(range(NCORES)))
    outs = []
    for i in range(NCORES):
        o = res.results[i]["out"]
        outs.append(o.transpose(0, 2, 1))
    return np.ascontiguousarray(np.concatenate(outs, axis=0))


if __name__ == "__main__":
    print("building program...")
    nc = build_nc()
    print("built ok")
